# version 09
# baseline (speedup 1.0000x reference)
import sys
if "/opt/trn_rl_repo" not in sys.path:
    sys.path.insert(0, "/opt/trn_rl_repo")
from contextlib import ExitStack

import numpy as np
import ml_dtypes

import concourse.bass as bass
import concourse.mybir as mybir
from concourse.bass_utils import run_bass_kernel_spmd

F32 = mybir.dt.float32
BF16 = mybir.dt.bfloat16
AF = mybir.ActivationFunctionType
ALU = mybir.AluOpType
AX = mybir.AxisListType.X

B, T, D = 4, 2048, 2048
HTOK = 1024
HD = 64

QLO = [0, 1, 2, 3, 8, 9, 10, 11]
QHI = [4, 5, 6, 7, 12, 13, 14, 15]
QPERM = []
for _p in range(8):
    QPERM += list(range(QLO[_p] * HD, QLO[_p] * HD + HD))
    QPERM += list(range(QHI[_p] * HD, QHI[_p] * HD + HD))
SLOT = {}
for _p in range(8):
    SLOT[QLO[_p]] = (_p, 0)
    SLOT[QHI[_p]] = (_p, 64)


class Sem:
    def __init__(self, ctx, nc, name):
        self.h = ctx.enter_context(nc.semaphore(name))
        self.n = 0

    def dma(self, instr):
        instr.then_inc(self.h, 16)
        self.n += 16
        return self.n

    def inc(self, instr, k=1):
        instr.then_inc(self.h, k)
        self.n += k
        return self.n


_VOPS = {"tensor_copy", "tensor_tensor", "tensor_add", "tensor_sub",
         "tensor_mul", "tensor_max", "tensor_scalar", "tensor_single_scalar",
         "tensor_scalar_add", "tensor_scalar_sub", "tensor_scalar_mul",
         "tensor_scalar_max", "tensor_scalar_min", "tensor_relu",
         "tensor_reduce", "reduce_max", "reduce_sum", "scalar_tensor_tensor",
         "tensor_tensor_scan", "reciprocal", "memset", "memzero", "select",
         "copy_predicated", "transpose", "iota", "bn_stats", "bn_aggr"}


class VW:

    def __init__(self, eng, sem):
        self._e = eng
        self._s = sem

    def __getattr__(self, name):
        fn = getattr(self._e, name)
        if name not in _VOPS:
            return fn
        sem = self._s
        eng = self._e

        def wrapped(*a, **k):
            eng.wait_ge(sem.h, sem.n)
            ins = fn(*a, **k)
            sem.inc(ins)
            return ins
        return wrapped


def host_stair():
    u = np.arange(896)[None, :]
    p = np.arange(128)[:, None]
    return np.where(u < p + 384, -1e9, 0.0).astype(np.float32)


def emit_ln_to_fm(nc, x_dram, dst_fm, nrows, ident, sq, pe, ve, fm_col0=0,
                  inner=False, ln_w=None, ln_b=None, keep_tm=None, x_row0=0,
                  x_is_sbuf=False):
    G = 4
    ntiles = nrows // 128
    ngroups = ntiles // G
    with ExitStack() as c2:
        sb = lambda *a: c2.enter_context(nc.sbuf_tensor(*a))
        xt = [sb(f"ln_x{i}", [128, D], F32) for i in range(G)]
        xn2 = [sb(f"ln_n{i}", [128, D], F32) for i in range(G)] if inner else None
        sq_s = sb("ln_sq", [128, D], F32)
        xnb = [sb(f"ln_b{i}", [128, D], BF16) for i in range(G)]
        sums = [sb(f"ln_su{i}", [128, 1], F32) for i in range(G)]
        sumsq = [sb(f"ln_sq{i}", [128, 1], F32) for i in range(G)]
        mean = [sb(f"ln_mn{i}", [128, 1], F32) for i in range(G)]
        var = [sb(f"ln_va{i}", [128, 1], F32) for i in range(G)]
        rstd = [sb(f"ln_rs{i}", [128, 1], F32) for i in range(G)]
        psbf = [c2.enter_context(nc.psum_tensor(f"ln_ps{i}", [128, 512], BF16))
                for i in range(2)]

        for g in range(ngroups):
            r0 = g * G
            if not x_is_sbuf:
                with nc.Block() as blk:
                    @blk.sync
                    def _(s):
                        for i in range(G):
                            row = x_row0 + (r0 + i) * 128
                            sq.dma(s.dma_start(xt[i][:], x_dram[row:row + 128, :]))
                        s.wait_ge(sq.h, sq.n)
                srcs = xt
            else:
                srcs = [x_dram[r0 + i] for i in range(G)]
            passes = 2 if inner else 1
            for pss in range(passes):
                last = pss == passes - 1
                dsts = xnb if last else xn2
                with nc.Block() as blk:
                    @blk.vector
                    def _(v):
                        v = VW(v, ve)
                        for i in range(G):
                            v.reduce_sum(sums[i][:], srcs[i][:], axis=AX)

                    @blk.scalar
                    def _(s):
                        for i in range(G):
                            s.activation(sq_s[:], srcs[i][:], AF.Square,
                                         accum_out=sumsq[i][:])
                with nc.Block() as blk:
                    @blk.vector
                    def _(v):
                        v = VW(v, ve)
                        for i in range(G):
                            v.tensor_single_scalar(mean[i][:], sums[i][:],
                                                   1.0 / D, ALU.mult)
                            v.tensor_mul(var[i][:], mean[i][:], mean[i][:])
                            v.scalar_tensor_tensor(var[i][:], sumsq[i][:],
                                                   1.0 / D, var[i][:],
                                                   ALU.mult, ALU.subtract)
                            v.tensor_single_scalar(var[i][:], var[i][:],
                                                   1e-5, ALU.add)
                with nc.Block() as blk:
                    @blk.scalar
                    def _(s):
                        for i in range(G):
                            s.activation(var[i][:], var[i][:], AF.Ln)
                            s.activation(rstd[i][:], var[i][:], AF.Exp,
                                         scale=-0.5)
                with nc.Block() as blk:
                    @blk.vector
                    def _(v):
                        v = VW(v, ve)
                        for i in range(G):
                            v.tensor_scalar(dsts[i][:], srcs[i][:],
                                            mean[i][:], rstd[i][:],
                                            ALU.subtract, ALU.mult)
                srcs = xn2
            if keep_tm is not None:
                with nc.Block() as blk:
                    @blk.vector
                    def _(v):
                        v = VW(v, ve)
                        for i in range(G):
                            v.tensor_copy(keep_tm[r0 + i][:], xnb[i][:])
            with nc.Block() as blk:
                pe0, ve0 = pe.n, ve.n

                @blk.tensor
                def _(t):
                    for i in range(G):
                        for j in range(16):
                            jj = i * 16 + j
                            bank = jj % 2
                            slot = (jj // 2) % 4
                            if jj >= 2:
                                t.wait_ge(ve.h, ve0 + jj - 1)
                            pe.inc(t.transpose(
                                psbf[bank][:, slot * 128:(slot + 1) * 128],
                                xnb[i][:, j * 128:(j + 1) * 128], ident[:]))

                @blk.vector
                def _(v):
                    v = VW(v, ve)
                    for i in range(G):
                        for j in range(16):
                            jj = i * 16 + j
                            bank = jj % 2
                            slot = (jj // 2) % 4
                            v.wait_ge(pe.h, pe0 + jj + 1)
                            col = fm_col0 + (r0 + i) * 128
                            src = psbf[bank][:, slot * 128:(slot + 1) * 128]
                            if ln_w is not None:
                                v.tensor_scalar(dst_fm[j][:, col:col + 128], src,
                                                ln_w[j][:], ln_b[j][:],
                                                ALU.mult, ALU.add)
                            else:
                                v.tensor_copy(dst_fm[j][:, col:col + 128], src)


def build_a1(upto=3, nqc=4, nheads=16, parts=7, F=None):
    nc = F if F is not None else bass.Bass("TRN2", target_bir_lowering=False)
    x0 = nc.dram_tensor("x0", [T, D], F32, kind="ExternalInput")
    wq_p = nc.dram_tensor("wq_p", [1024, D], F32, kind="ExternalInput")
    wk_h = nc.dram_tensor("wk_h", [256, D], F32, kind="ExternalInput")
    wv_h = nc.dram_tensor("wv_h", [256, D], F32, kind="ExternalInput")
    ident_d = nc.dram_tensor("ident", [128, 128], BF16, kind="ExternalInput")
    stair_d = nc.dram_tensor("stair", [128, 896], F32, kind="ExternalInput")

    attn_out = nc.dram_tensor("attn_out", [1024, T], BF16, kind="ExternalOutput")
    xn0_out = nc.dram_tensor("xn0_out", [D, T], BF16, kind="ExternalOutput")

    wq_bf = nc.dram_tensor("wq_bf", [1024, D], BF16, kind="Internal")
    wk_bf = nc.dram_tensor("wk_bf", [256, D], BF16, kind="Internal")
    wv_bf = nc.dram_tensor("wv_bf", [256, D], BF16, kind="Internal")

    with ExitStack() as ctx:
        sb = lambda *a: ctx.enter_context(nc.sbuf_tensor(*a))
        gq = Sem(ctx, nc, "gq")
        sq = Sem(ctx, nc, "sq")
        pe = Sem(ctx, nc, "pe")
        ve = Sem(ctx, nc, "ve")
        se = Sem(ctx, nc, "se")

        xn0_fm = [sb(f"xn0_{p}", [128, T], BF16) for p in range(16)]
        ident = sb("identb", [128, 128], BF16)
        stair = sb("stairb", [128, 896], F32)

        with nc.Block() as blk:
            @blk.gpsimd
            def _(g):
                gq.dma(g.dma_start(wq_bf[:], wq_p[:]))
                gq.dma(g.dma_start(wk_bf[:], wk_h[:]))
                gq.dma(g.dma_start(wv_bf[:], wv_h[:]))
                gq.dma(g.dma_start(ident[:], ident_d[:]))
                gq.dma(g.dma_start(stair[:], stair_d[:]))
                g.wait_ge(gq.h, gq.n)

        emit_ln_to_fm(nc, x0, xn0_fm, T, ident, sq, pe, ve)

        with nc.Block() as blk:
            @blk.gpsimd
            def _(g):
                for p in range(16):
                    gq.dma(g.dma_start(xn0_out[p * 128:(p + 1) * 128, :],
                                       xn0_fm[p][:]))

        wqT = [sb(f"wqT_{k}", [128, 1024], BF16) for k in range(16)]
        wkT = [sb(f"wkT_{k}", [128, 256], BF16) for k in range(16)]
        wvT = [sb(f"wvT_{k}", [128, 256], BF16) for k in range(16)]
        kfm = [sb(f"kfm_{p}", [128, T], BF16) for p in range(2)]
        vbuf = [sb(f"vbuf_{k}", [128, 260], BF16) for k in range(16)]
        qfm = [sb(f"qfm_{p}", [128, 512], BF16) for p in range(8)]
        attn_fm = [sb(f"attn_{p}", [128, T], BF16) for p in range(8)]
        onesb = sb("onesb", [128, 128], F32)
        sraw = [sb(f"sraw{i}", [128, 16 * 512], BF16) for i in range(2)]
        rrow = sb("rrow", [128, 512], F32)
        araw = sb("araw", [128, 512], BF16)

        with nc.Block() as blk:
            @blk.sync
            def _(s):
                for kp in range(16):
                    sq.dma(s.dma_start_transpose(
                        wqT[kp][:], wq_bf[:, kp * 128:(kp + 1) * 128]))
                for kp in range(16):
                    sq.dma(s.dma_start_transpose(
                        wkT[kp][:], wk_bf[:, kp * 128:(kp + 1) * 128]))
                for kp in range(16):
                    sq.dma(s.dma_start_transpose(
                        wvT[kp][:], wv_bf[:, kp * 128:(kp + 1) * 128]))
                s.wait_ge(sq.h, sq.n)

            @blk.gpsimd
            def _(g):
                gv = VW(g, gq)
                gv.memset(onesb[:], 1.0)
                for kk in range(16):
                    gv.memset(vbuf[kk][:], 1.0)

        if upto < 2:
            return nc
        with ExitStack() as c2:
            ps = [c2.enter_context(nc.psum_tensor(f"kv_ps{i}", [128, 512], F32))
                  for i in range(4)]
            with nc.Block() as blk:
                pe0, ve0 = pe.n, ve.n
                jobs = [("k", m, tc) for m in range(2) for tc in range(4)]
                jobs += [("v", kt, None) for kt in range(16)]

                @blk.tensor
                def _(t):
                    for mi, (kind, a1, a2) in enumerate(jobs):
                        bank = mi % 4
                        if mi >= 4:
                            t.wait_ge(ve.h, ve0 + mi - 3)
                        if kind == "k":
                            m, tc = a1, a2
                            for kp in range(16):
                                mm = t.matmul(
                                    ps[bank][:],
                                    wkT[kp][:, m * 128:(m + 1) * 128],
                                    xn0_fm[kp][:, tc * 512:(tc + 1) * 512],
                                    start=(kp == 0), stop=(kp == 15))
                        else:
                            kt = a1
                            for kp in range(16):
                                mm = t.matmul(
                                    ps[bank][:, 0:256],
                                    xn0_fm[kp][:, kt * 128:(kt + 1) * 128],
                                    wvT[kp][:],
                                    start=(kp == 0), stop=(kp == 15))
                        pe.inc(mm)

                @blk.vector
                def _(v):
                    v = VW(v, ve)
                    for mi, (kind, a1, a2) in enumerate(jobs):
                        bank = mi % 4
                        v.wait_ge(pe.h, pe0 + mi + 1)
                        if kind == "k":
                            m, tc = a1, a2
                            v.tensor_copy(kfm[m][:, tc * 512:(tc + 1) * 512],
                                          ps[bank][:])
                        else:
                            kt = a1
                            dst = bass.AP(vbuf[kt], 0,
                                          [[260, 128], [65, 4], [1, 64]])
                            src = ps[bank].ap()[:, 0:256].rearrange(
                                "p (g c) -> p g c", g=4)
                            v.tensor_copy(dst, src)

        if upto < 3:
            return nc
        with ExitStack() as c2:
            ps = [c2.enter_context(nc.psum_tensor(f"at_ps{i}", [128, 512], F32))
                  for i in range(5)]
            for qc in range(nqc):
                KT = qc * 4 + 4
                diag0 = qc * 4
                with nc.Block() as blk:
                    pe0, ve0 = pe.n, ve.n

                    @blk.tensor
                    def _(t):
                        for m in range(8):
                            bank = m % 4
                            if m >= 4:
                                t.wait_ge(ve.h, ve0 + m - 3)
                            for kp in range(16):
                                mm = t.matmul(
                                    ps[bank][:],
                                    wqT[kp][:, m * 128:(m + 1) * 128],
                                    xn0_fm[kp][:, qc * 512:(qc + 1) * 512],
                                    start=(kp == 0), stop=(kp == 15))
                            pe.inc(mm)

                    @blk.vector
                    def _(v):
                        v = VW(v, ve)
                        for m in range(8):
                            bank = m % 4
                            v.wait_ge(pe.h, pe0 + m + 1)
                            v.tensor_copy(qfm[m][:], ps[bank][:])

                for h in range(nheads + 1):
                    if parts & 1 == 0:
                        continue
                    with nc.Block() as blk:
                        pe0, ve0 = pe.n, ve.n
                        if h < nheads:
                            kvh = h // 4
                            kv_pt, kv_base = kvh // 2, (kvh % 2) * 64
                            q_pt, q_base = SLOT[h]
                            dve_marks = []
                            nops = 0
                            for kt in range(KT):
                                nops += 2 if kt >= diag0 else 1
                                dve_marks.append(nops)

                            @blk.tensor
                            def _(t):
                                for kt in range(KT):
                                    if kt >= 2:
                                        t.wait_ge(ve.h, ve0 + dve_marks[kt - 2])
                                    pe.inc(t.matmul(
                                        ps[kt % 2][:],
                                        kfm[kv_pt][kv_base:kv_base + 64,
                                                   kt * 128:(kt + 1) * 128],
                                        qfm[q_pt][q_base:q_base + 64, :],
                                        start=True, stop=True))

                            @blk.vector
                            def _(v):
                                v = VW(v, ve)
                                for kt in range(KT):
                                    v.wait_ge(pe.h, pe0 + kt + 1)
                                    if kt >= diag0:
                                        j = kt - diag0
                                        v.tensor_add(
                                            ps[kt % 2][:], ps[kt % 2][:],
                                            stair[:, 384 - j * 128:
                                                  896 - j * 128])
                                    v.tensor_copy(
                                        sraw[h % 2][:, kt * 512:(kt + 1) * 512],
                                        ps[kt % 2][:])
                        if h >= 1:
                            @blk.scalar
                            def _(sc):
                                sc.activation(sraw[(h - 1) % 2][:, 0:KT * 512],
                                              sraw[(h - 1) % 2][:, 0:KT * 512],
                                              AF.Exp, scale=0.125)
                    if h >= 1:
                        hp = h - 1
                        kvh = hp // 4
                        av = 2 + (hp % 2)
                        with nc.Block() as blk:
                            pe0, ve0 = pe.n, ve.n

                            @blk.tensor
                            def _(t):
                                for kt in range(KT):
                                    mm = t.matmul(
                                        ps[av][0:65, :],
                                        vbuf[kt][:, kvh * 65: kvh * 65 + 65],
                                        sraw[hp % 2][:, kt * 512:(kt + 1) * 512],
                                        start=(kt == 0), stop=(kt == KT - 1))
                                pe.inc(mm)
                                t.wait_ge(ve.h, ve0 + 1)
                                pe.inc(t.matmul(ps[4][0:64, :],
                                                onesb[0:1, 0:64],
                                                rrow[0:1, :],
                                                start=True, stop=True))

                            @blk.vector
                            def _(v):
                                v = VW(v, ve)
                                v.wait_ge(pe.h, pe0 + 1)
                                v.reciprocal(rrow[0:1, :], ps[av][64:65, :])
                                v.tensor_copy(araw[0:64, :], ps[av][0:64, :])
                                v.wait_ge(pe.h, pe0 + 2)
                                v.tensor_mul(
                                    attn_fm[hp // 2][(hp % 2) * 64:
                                                     (hp % 2) * 64 + 64,
                                                     qc * 512:(qc + 1) * 512],
                                    araw[0:64, :], ps[4][0:64, :])

        with nc.Block() as blk:
            @blk.gpsimd
            def _(g):
                for p in range(8):
                    gq.dma(g.dma_start(attn_out[p * 128:(p + 1) * 128, :],
                                       attn_fm[p][:]))
                g.wait_ge(gq.h, gq.n)
    return nc



def _in_maps_a1(inputs):
    x = np.asarray(inputs["L0_attn_x"])
    wq = np.asarray(inputs["L0_attn_wq"])
    wk = np.asarray(inputs["L0_attn_wk"])
    wv = np.asarray(inputs["L0_attn_wv"])
    ident = np.eye(128, dtype=np.float32).astype(ml_dtypes.bfloat16)
    stair = host_stair()
    in_maps = []
    for c in range(8):
        b, half = c // 2, c % 2
        wq_h = wq[half * 1024:(half + 1) * 1024][QPERM]
        in_maps.append({
            "x0": np.ascontiguousarray(x[b]),
            "wq_p": np.ascontiguousarray(wq_h),
            "wk_h": np.ascontiguousarray(wk[half * 256:(half + 1) * 256]),
            "wv_h": np.ascontiguousarray(wv[half * 256:(half + 1) * 256]),
            "ident": ident,
            "stair": stair,
        })
    return in_maps


def run_a1(inputs, upto=3, nqc=4, nheads=16, parts=7):
    nc = build_a1(upto, nqc, nheads, parts)
    res = run_bass_kernel_spmd(nc, _in_maps_a1(inputs), core_ids=list(range(8)))
    return res.results


def kernel(**inputs):
    return kernel_fused(**inputs)


def build_a2(upto=6, nexp=8, parts=7, gmode=0, F=None):
    nc = F if F is not None else bass.Bass("TRN2", target_bir_lowering=False)
    precast = getattr(F, "precast", False)
    xn0f_d = nc.dram_tensor("xn0f", [D, HTOK], BF16, kind="ExternalInput")
    attn_d = nc.dram_tensor("attnf", [D, HTOK], BF16, kind="ExternalInput")
    x0_d = nc.dram_tensor("x0own", [HTOK, D], F32, kind="ExternalInput")
    wo_d = nc.dram_tensor("wo", [D, D], F32, kind="ExternalInput")
    gw_d = nc.dram_tensor("gate_w", [D, 8], F32, kind="ExternalInput")
    w1_d = nc.dram_tensor("w1", [8, D, 1024], F32, kind="ExternalInput")
    b1_d = nc.dram_tensor("b1c", [8192, 1], F32, kind="ExternalInput")
    w2_d = nc.dram_tensor("w2", [8, 1024, D], F32, kind="ExternalInput")
    b2e_d = nc.dram_tensor("b2ext", [1, 9 * D], F32, kind="ExternalInput")
    sw1_d = nc.dram_tensor("sw1", [D, D], F32, kind="ExternalInput")
    sb1c_d = nc.dram_tensor("sb1c", [D, 1], F32, kind="ExternalInput")
    sw2_d = nc.dram_tensor("sw2", [D, D], F32, kind="ExternalInput")


    x1_out = nc.dram_tensor("x1_out", [HTOK, D], F32, kind="ExternalOutput")

    wo_bf = nc.dram_tensor("wo_bf", [D, D], BF16, kind="Internal")
    s0_d = nc.dram_tensor("s0_d", [HTOK, D], BF16, kind="Internal")

    with ExitStack() as ctx:
        sb = lambda *a: ctx.enter_context(nc.sbuf_tensor(*a))
        gq = Sem(ctx, nc, "gq")
        sq = Sem(ctx, nc, "sq")
        pe = Sem(ctx, nc, "pe")
        ve = Sem(ctx, nc, "ve")
        se = Sem(ctx, nc, "se")

        xn0 = [sb(f"xn0_{p}", [128, HTOK], BF16) for p in range(16)]
        gates = [sb(f"gates_{t}", [128, 8], F32) for t in range(8)]
        gwb = [sb(f"gwb_{p}", [128, 8], BF16) for p in range(16)]
        sb1c = [sb(f"sb1c_{m}", [128, 1], F32) for m in range(16)]
        onesr = sb("onesr", [1, HTOK], BF16)

        with nc.Block() as blk:
            @blk.gpsimd
            def _(g):
                gv = VW(g, gq)
                for p in range(16):
                    gq.dma(g.dma_start(xn0[p][:], xn0f_d[p * 128:(p + 1) * 128, :]))
                if not precast:
                    gq.dma(g.dma_start(wo_bf[:], wo_d[:]))
                for p in range(16):
                    gq.dma(g.dma_start(gwb[p][:], gw_d[p * 128:(p + 1) * 128, :]))
                for m in range(16):
                    gq.dma(g.dma_start(sb1c[m][:], sb1c_d[m * 128:(m + 1) * 128, :]))
                gv.memset(onesr[:], 1.0)
                g.wait_ge(gq.h, gq.n)

        with ExitStack() as c2:
            woT = [c2.enter_context(nc.sbuf_tensor(f"woT_{k}", [128, D], BF16))
                   for k in range(16)]
            attn = [c2.enter_context(nc.sbuf_tensor(f"at_{k}", [128, HTOK], BF16))
                    for k in range(16)]
            s0t = [c2.enter_context(nc.sbuf_tensor(f"s0_{t}", [128, D], BF16))
                   for t in range(2)]
            ps = [c2.enter_context(nc.psum_tensor(f"wo_ps{i}", [128, 512], F32))
                  for i in range(8)]
            with nc.Block() as blk:
                @blk.sync
                def _(s):
                    for kp in range(16):
                        sq.dma(s.dma_start_transpose(
                            woT[kp][:], wo_bf[:, kp * 128:(kp + 1) * 128]))
                    s.wait_ge(sq.h, sq.n)

                @blk.gpsimd
                def _(g):
                    for kp in range(16):
                        gq.dma(g.dma_start(attn[kp][:],
                                           attn_d[kp * 128:(kp + 1) * 128, :]))
                    g.wait_ge(gq.h, gq.n)

            with nc.Block() as blk:
                pe0, ve0, gq0 = pe.n, ve.n, gq.n

                @blk.tensor
                def _(t):
                    for tt in range(8):
                        bs = (tt % 2) * 4
                        if tt >= 2:
                            t.wait_ge(ve.h, ve0 + (tt - 2) * 4 + 4)
                        for kp in range(16):
                            for oc in range(4):
                                mm = t.matmul(
                                    ps[bs + oc][:],
                                    attn[kp][:, tt * 128:(tt + 1) * 128],
                                    woT[kp][:, oc * 512:(oc + 1) * 512],
                                    start=(kp == 0), stop=(kp == 15))
                        pe.inc(mm)

                @blk.vector
                def _(v):
                    v = VW(v, ve)
                    for tt in range(8):
                        bs = (tt % 2) * 4
                        v.wait_ge(pe.h, pe0 + tt + 1)
                        if tt >= 2:
                            v.wait_ge(gq.h, gq0 + (tt - 1) * 16)
                        for oc in range(4):
                            v.tensor_copy(s0t[tt % 2][:, oc * 512:(oc + 1) * 512],
                                          ps[bs + oc][:])

                @blk.gpsimd
                def _(g):
                    for tt in range(8):
                        g.wait_ge(ve.h, ve0 + tt * 4 + 4)
                        gq.dma(g.dma_start(s0_d[tt * 128:(tt + 1) * 128, :],
                                           s0t[tt % 2][:]))
                        g.wait_ge(gq.h, gq.n)

        acc = [sb(f"acc_{t}", [128, D], F32) for t in range(8)]
        with nc.Block() as blk:
            @blk.gpsimd
            def _(g):
                gv = VW(g, gq)
                for t in range(8):
                    gv.memset(acc[t][:], 0.0)

        if upto < 2:
            return nc
        with ExitStack() as c2:
            ps = [c2.enter_context(nc.psum_tensor("g_ps", [128, 64], F32))]
            lg = [c2.enter_context(nc.sbuf_tensor(f"lg{t}", [128, 8], F32))
                  for t in range(8)]
            m1 = [c2.enter_context(nc.sbuf_tensor(f"m1_{t}", [128, 1], F32))
                  for t in range(8)]
            nm1 = [c2.enter_context(nc.sbuf_tensor(f"nm1_{t}", [128, 1], F32))
                   for t in range(8)]
            ee = [c2.enter_context(nc.sbuf_tensor(f"ee{t}", [128, 8], F32))
                  for t in range(8)]
            scr = [c2.enter_context(nc.sbuf_tensor(f"sc{t}", [128, 8], F32))
                   for t in range(8)]
            zz = [c2.enter_context(nc.sbuf_tensor(f"zz{t}", [128, 1], F32))
                  for t in range(8)]
            with nc.Block() as blk:
                pe0, ve0 = pe.n, ve.n

                @blk.tensor
                def _(t):
                    for tt in range(8):
                        for kp in range(16):
                            mm = t.matmul(ps[0][:, tt * 8:(tt + 1) * 8],
                                          xn0[kp][:, tt * 128:(tt + 1) * 128],
                                          gwb[kp][:],
                                          start=(tt == 0 and kp == 0),
                                          stop=(tt == 7 and kp == 15))
                    pe.inc(mm)

                @blk.vector
                def _(v):
                    v = VW(v, ve)
                    v.wait_ge(pe.h, pe0 + 1)
                    for tt in range(8):
                        v.tensor_copy(lg[tt][:], ps[0][:, tt * 8:(tt + 1) * 8])
                        v.reduce_max(m1[tt][:], lg[tt][:], axis=AX)
                        v.tensor_single_scalar(nm1[tt][:], m1[tt][:], -1.0,
                                               ALU.mult)
            with nc.Block() as blk:
                @blk.scalar
                def _(s):
                    for tt in range(8):
                        s.activation(ee[tt][:], lg[tt][:], AF.Exp,
                                     bias=nm1[tt][:])
            with nc.Block() as blk:
                pe0, ve0, se0 = pe.n, ve.n, se.n

                @blk.vector
                def _(v):
                    v = VW(v, ve)
                    for tt in range(8):
                        v.tensor_single_scalar(scr[tt][:], lg[tt][:], m1[tt][:],
                                               ALU.is_ge)
                        v.scalar_tensor_tensor(scr[tt][:], scr[tt][:], -1e9,
                                               lg[tt][:], ALU.mult, ALU.add)
                        v.reduce_max(zz[tt][:], scr[tt][:], axis=AX)
                        v.tensor_single_scalar(scr[tt][:], lg[tt][:], zz[tt][:],
                                               ALU.is_ge)
                        v.tensor_mul(scr[tt][:], scr[tt][:], ee[tt][:])
                        v.reduce_sum(zz[tt][:], scr[tt][:], axis=AX)
                        v.reciprocal(zz[tt][:], zz[tt][:])
                        v.tensor_single_scalar(gates[tt][:], scr[tt][:],
                                               zz[tt][:], ALU.mult)

        if upto < 3:
            return nc
        with ExitStack() as c2:
            w1s = [c2.enter_context(nc.sbuf_tensor(f"w1s_{k}", [128, 1024], BF16))
                   for k in range(16)]
            w2s = [c2.enter_context(nc.sbuf_tensor(f"w2s_{m}", [128, D], BF16))
                   for m in range(8)]
            act2 = [[c2.enter_context(nc.sbuf_tensor(f"act_{i}_{m}",
                                                     [128, HTOK], BF16))
                     for m in range(8)] for i in range(2)]
            b1c2 = [[c2.enter_context(nc.sbuf_tensor(f"b1cc_{i}_{m}",
                                                     [128, 1], F32))
                     for m in range(8)] for i in range(2)]
            b2t2 = [c2.enter_context(nc.sbuf_tensor(f"b2tt_{i}", [1, D], BF16))
                    for i in range(2)]
            ps = [c2.enter_context(nc.psum_tensor(f"ff_ps{i}", [128, 512], F32))
                  for i in range(8)]
            with nc.Block() as blk:
                @blk.gpsimd
                def _(g):
                    for kp in range(16):
                        gq.dma(g.dma_start(
                            w1s[kp][:], w1_d[0, kp * 128:(kp + 1) * 128, :]))
                    g.wait_ge(gq.h, gq.n)
            for e in range(nexp + 1):
                if e < nexp:
                    with nc.Block() as blk:
                        pe0, ve0 = pe.n, ve.n
                        jobs = [(m, tc) for m in range(8) for tc in range(2)]

                        @blk.gpsimd
                        def _(g):
                            if e >= 1:
                                for m in range(8):
                                    gq.dma(g.dma_start(
                                        w2s[m][:],
                                        w2_d[e - 1, m * 128:(m + 1) * 128, :]))
                            for m in range(8):
                                gq.dma(g.dma_start(
                                    b1c2[e % 2][m][:],
                                    b1_d[e * 1024 + m * 128:
                                         e * 1024 + (m + 1) * 128, :]))
                            gq.dma(g.dma_start(b2t2[e % 2][:],
                                               b2e_d[0:1, e * D:(e + 1) * D]))
                            g.wait_ge(gq.h, gq.n)

                        @blk.tensor
                        def _(t):
                            for ji, (m, tc) in enumerate(jobs):
                                bank = ji % 8
                                if ji >= 8:
                                    t.wait_ge(ve.h, ve0 + ji - 7)
                                for kp in range(16):
                                    mm = t.matmul(
                                        ps[bank][:],
                                        w1s[kp][:, m * 128:(m + 1) * 128],
                                        xn0[kp][:, tc * 512:(tc + 1) * 512],
                                        start=(kp == 0), stop=(kp == 15))
                                pe.inc(mm)

                        @blk.vector
                        def _(v):
                            vv = VW(v, ve)
                            for ji, (m, tc) in enumerate(jobs):
                                bank = ji % 8
                                v.wait_ge(pe.h, pe0 + ji + 1)
                                vv.tensor_copy(
                                    act2[e % 2][m][:, tc * 512:(tc + 1) * 512],
                                    ps[bank][:])

                        if e >= 1:
                            @blk.scalar
                            def _(sc):
                                for m in range(8):
                                    sc.activation(act2[(e - 1) % 2][m][:],
                                                  act2[(e - 1) % 2][m][:],
                                                  AF.Gelu,
                                                  bias=b1c2[(e - 1) % 2][m][:])
                else:
                    with nc.Block() as blk:
                        @blk.gpsimd
                        def _(g):
                            for m in range(8):
                                gq.dma(g.dma_start(
                                    w2s[m][:],
                                    w2_d[e - 1, m * 128:(m + 1) * 128, :]))
                            g.wait_ge(gq.h, gq.n)

                        @blk.scalar
                        def _(sc):
                            for m in range(8):
                                sc.activation(act2[(e - 1) % 2][m][:],
                                              act2[(e - 1) % 2][m][:],
                                              AF.Gelu,
                                              bias=b1c2[(e - 1) % 2][m][:])
                if e >= 1:
                    ep = e - 1
                    with nc.Block() as blk:
                        pe0, ve0 = pe.n, ve.n
                        jobs = [(tt, oc) for tt in range(8) for oc in range(4)]

                        @blk.gpsimd
                        def _(g):
                            if e + 1 < nexp + 1 and e < nexp:
                                for kp in range(16):
                                    gq.dma(g.dma_start(
                                        w1s[kp][:],
                                        w1_d[e, kp * 128:(kp + 1) * 128, :]))
                                g.wait_ge(gq.h, gq.n)

                        @blk.tensor
                        def _(t):
                            for ji, (tt, oc) in enumerate(jobs):
                                bank = ji % 8
                                if ji >= 8:
                                    t.wait_ge(ve.h, ve0 + ji - 7)
                                for m in range(8):
                                    t.matmul(
                                        ps[bank][:],
                                        act2[ep % 2][m][:, tt * 128:(tt + 1) * 128],
                                        w2s[m][:, oc * 512:(oc + 1) * 512],
                                        start=(m == 0), stop=False)
                                mm = t.matmul(
                                    ps[bank][:],
                                    onesr[0:1, tt * 128:(tt + 1) * 128],
                                    b2t2[ep % 2][0:1, oc * 512:(oc + 1) * 512],
                                    start=False, stop=True)
                                pe.inc(mm)

                        @blk.vector
                        def _(v):
                            vv = VW(v, ve)
                            for ji, (tt, oc) in enumerate(jobs):
                                bank = ji % 8
                                v.wait_ge(pe.h, pe0 + ji + 1)
                                vv.scalar_tensor_tensor(
                                    acc[tt][:, oc * 512:(oc + 1) * 512],
                                    ps[bank][:], gates[tt][:, ep:ep + 1],
                                    acc[tt][:, oc * 512:(oc + 1) * 512],
                                    ALU.mult, ALU.add)

        with ExitStack() as c2:
            sws = [c2.enter_context(nc.sbuf_tensor(f"sws_{k}", [128, D], BF16))
                   for k in range(16)]
            b2t = c2.enter_context(nc.sbuf_tensor("b2t", [1, D], BF16))
            acs = [c2.enter_context(nc.sbuf_tensor(f"acs_{m}", [128, HTOK], BF16))
                   for m in range(16)]
            ps = [c2.enter_context(nc.psum_tensor(f"sh_ps{i}", [128, 512], F32))
                  for i in range(8)]
            with nc.Block() as blk:
                @blk.gpsimd
                def _(g):
                    for kp in range(16):
                        gq.dma(g.dma_start(sws[kp][:],
                                           sw1_d[kp * 128:(kp + 1) * 128, :]))
                    g.wait_ge(gq.h, gq.n)
            with nc.Block() as blk:
                pe0, ve0 = pe.n, ve.n
                jobs = [(m, tc) for m in range(16) for tc in range(2)]

                @blk.tensor
                def _(t):
                    for ji, (m, tc) in enumerate(jobs):
                        bank = ji % 8
                        if ji >= 8:
                            t.wait_ge(ve.h, ve0 + ji - 7)
                        for kp in range(16):
                            mm = t.matmul(ps[bank][:],
                                          sws[kp][:, m * 128:(m + 1) * 128],
                                          xn0[kp][:, tc * 512:(tc + 1) * 512],
                                          start=(kp == 0), stop=(kp == 15))
                        pe.inc(mm)

                @blk.vector
                def _(v):
                    vv = VW(v, ve)
                    for ji, (m, tc) in enumerate(jobs):
                        bank = ji % 8
                        v.wait_ge(pe.h, pe0 + ji + 1)
                        vv.tensor_copy(acs[m][:, tc * 512:(tc + 1) * 512],
                                       ps[bank][:])
            with nc.Block() as blk:
                @blk.scalar
                def _(sc):
                    for m in range(16):
                        sc.activation(acs[m][:], acs[m][:], AF.Gelu,
                                      bias=sb1c[m][:])
            with nc.Block() as blk:
                @blk.gpsimd
                def _(g):
                    for kp in range(16):
                        gq.dma(g.dma_start(sws[kp][:],
                                           sw2_d[kp * 128:(kp + 1) * 128, :]))
                    gq.dma(g.dma_start(b2t[:], b2e_d[0:1, 8 * D:9 * D]))
                    g.wait_ge(gq.h, gq.n)
            with nc.Block() as blk:
                pe0, ve0 = pe.n, ve.n
                jobs = [(tt, oc) for tt in range(8) for oc in range(4)]

                @blk.tensor
                def _(t):
                    for ji, (tt, oc) in enumerate(jobs):
                        bank = ji % 8
                        if ji >= 8:
                            t.wait_ge(ve.h, ve0 + ji - 7)
                        for m in range(16):
                            t.matmul(ps[bank][:],
                                     acs[m][:, tt * 128:(tt + 1) * 128],
                                     sws[m][:, oc * 512:(oc + 1) * 512],
                                     start=(m == 0), stop=False)
                        mm = t.matmul(
                            ps[bank][:],
                            onesr[0:1, tt * 128:(tt + 1) * 128],
                            b2t[0:1, oc * 512:(oc + 1) * 512],
                            start=False, stop=True)
                        pe.inc(mm)

                @blk.vector
                def _(v):
                    v = VW(v, ve)
                    for ji, (tt, oc) in enumerate(jobs):
                        bank = ji % 8
                        v.wait_ge(pe.h, pe0 + ji + 1)
                        v.tensor_add(acc[tt][:, oc * 512:(oc + 1) * 512],
                                     acc[tt][:, oc * 512:(oc + 1) * 512],
                                     ps[bank][:])

        if upto < 6:
            return nc
        with ExitStack() as c2:
            g0t = c2.enter_context(nc.sbuf_tensor("g0t", [128, D], F32))
            s0b = c2.enter_context(nc.sbuf_tensor("s0b", [128, D], BF16))
            x0t = c2.enter_context(nc.sbuf_tensor("x0t", [128, D], F32))
            x1t = c2.enter_context(nc.sbuf_tensor("x1t", [128, D], F32))
            for tt in range(8):
                with nc.Block() as blk:
                    @blk.sync
                    def _(s):
                        sq.dma(s.dma_start(s0b[:], s0_d[tt * 128:(tt + 1) * 128, :]))
                        sq.dma(s.dma_start(x0t[:], x0_d[tt * 128:(tt + 1) * 128, :]))
                        s.wait_ge(sq.h, sq.n)

                    @blk.scalar
                    def _(s):
                        s.activation(g0t[:], acc[tt][:], AF.Silu)
                with nc.Block() as blk:
                    @blk.vector
                    def _(v):
                        v = VW(v, ve)
                        v.tensor_mul(g0t[:], g0t[:], s0b[:])
                        v.tensor_add(x1t[:], x0t[:], g0t[:])
                with nc.Block() as blk:
                    @blk.gpsimd
                    def _(g):
                        gq.dma(g.dma_start(x1_out[tt * 128:(tt + 1) * 128, :],
                                           x1t[:]))
                        g.wait_ge(gq.h, gq.n)
    return nc


def _in_maps_a2(inputs, a1_res):
    x = np.asarray(inputs["L0_attn_x"])
    ident = np.eye(128, dtype=np.float32).astype(ml_dtypes.bfloat16)
    base = {
        "wo": np.asarray(inputs["L0_attn_wo"]),
        "gate_w": np.asarray(inputs["L0_ffn_gate_w"]),
        "w1": np.asarray(inputs["L0_ffn_w1"]),
        "b1c": np.ascontiguousarray(np.asarray(inputs["L0_ffn_b1"]).reshape(-1, 1)),
        "w2": np.asarray(inputs["L0_ffn_w2"]),
        "b2ext": np.ascontiguousarray(np.concatenate(
            [np.asarray(inputs["L0_ffn_b2"]).reshape(-1),
             np.asarray(inputs["L0_ffn_shared_b2"]).reshape(-1)])[None, :]),
        "sw1": np.asarray(inputs["L0_ffn_shared_w1"]),
        "sb1c": np.ascontiguousarray(
            np.asarray(inputs["L0_ffn_shared_b1"])[:, None]),
        "sw2": np.asarray(inputs["L0_ffn_shared_w2"]),
    }
    in_maps = []
    for c in range(8):
        b, half = c // 2, c % 2
        cols = slice(half * 1024, half * 1024 + 1024)
        attn_full = np.concatenate(
            [np.asarray(a1_res[2 * b]["attn_out"])[:, cols],
             np.asarray(a1_res[2 * b + 1]["attn_out"])[:, cols]], axis=0)
        m = dict(base)
        m["xn0f"] = np.ascontiguousarray(
            np.asarray(a1_res[2 * b + half]["xn0_out"])[:, cols])
        m["attnf"] = np.ascontiguousarray(attn_full)
        m["x0own"] = np.ascontiguousarray(x[b, half * 1024:half * 1024 + 1024])
        in_maps.append(m)
    return in_maps


def run_a2(inputs, a1_res, upto=6, nexp=8, parts=7, gmode=0):
    nc = build_a2(upto, nexp, parts, gmode)
    res = run_bass_kernel_spmd(nc, _in_maps_a2(inputs, a1_res),
                               core_ids=list(range(8)))
    return res.results


def build_b(F=None):
    nc = F if F is not None else bass.Bass("TRN2", target_bir_lowering=False)
    precast = getattr(F, "precast", False)
    x1_d = nc.dram_tensor("x1f", [T, D], F32, kind="ExternalInput")
    wk_d = nc.dram_tensor("wk1", [1024, D], F32, kind="ExternalInput")
    wv_d = nc.dram_tensor("wv1", [1024, D], F32, kind="ExternalInput")
    lnw_d = nc.dram_tensor("lnw", [D, 1], F32, kind="ExternalInput")
    lnb_d = nc.dram_tensor("lnb", [D, 1], F32, kind="ExternalInput")
    mk_d = nc.dram_tensor("mixk", [D, 1], F32, kind="ExternalInput")
    mv_d = nc.dram_tensor("mixv", [D, 1], F32, kind="ExternalInput")
    td_d = nc.dram_tensor("tdec", [1024, 1], F32, kind="ExternalInput")
    cw_d = nc.dram_tensor("ctw", [1024, 1], F32, kind="ExternalInput")
    ident_d = nc.dram_tensor("ident", [128, 128], BF16, kind="ExternalInput")

    xln_out = nc.dram_tensor("xln_out", [D, T + 1], BF16, kind="ExternalOutput")
    wkv_out = nc.dram_tensor("wkv_out", [1024, T], BF16, kind="ExternalOutput")

    wk_bf = nc.dram_tensor("wk_bf", [1024, D], BF16, kind="Internal")
    wv_bf = nc.dram_tensor("wv_bf", [1024, D], BF16, kind="Internal")

    with ExitStack() as ctx:
        sb = lambda *a: ctx.enter_context(nc.sbuf_tensor(*a))
        gq = Sem(ctx, nc, "gq")
        sq = Sem(ctx, nc, "sq")
        pe = Sem(ctx, nc, "pe")
        ve = Sem(ctx, nc, "ve")
        se = Sem(ctx, nc, "se")

        xln = [sb(f"xln_{p}", [128, T + 1], BF16) for p in range(16)]
        ident = sb("identb", [128, 128], BF16)
        lnw = [sb(f"lnw_{p}", [128, 1], F32) for p in range(16)]
        lnb = [sb(f"lnb_{p}", [128, 1], F32) for p in range(16)]
        mk = [sb(f"mk_{p}", [128, 1], F32) for p in range(16)]
        mv = [sb(f"mv_{p}", [128, 1], F32) for p in range(16)]
        tdc = [sb(f"td_{s}", [128, 1], F32) for s in range(8)]
        cwc = [sb(f"cw_{s}", [128, 1], F32) for s in range(8)]
        acol = [sb(f"ac_{s}", [128, 1], F32) for s in range(8)]

        with nc.Block() as blk:
            @blk.gpsimd
            def _(g):
                gv = VW(g, gq)
                if not precast:
                    gq.dma(g.dma_start(wk_bf[:], wk_d[:]))
                    gq.dma(g.dma_start(wv_bf[:], wv_d[:]))
                gq.dma(g.dma_start(ident[:], ident_d[:]))
                for p in range(16):
                    sl = slice(p * 128, (p + 1) * 128)
                    gq.dma(g.dma_start(lnw[p][:], lnw_d[sl, :]))
                    gq.dma(g.dma_start(lnb[p][:], lnb_d[sl, :]))
                    gq.dma(g.dma_start(mk[p][:], mk_d[sl, :]))
                    gq.dma(g.dma_start(mv[p][:], mv_d[sl, :]))
                for s_ in range(8):
                    sl = slice(s_ * 128, (s_ + 1) * 128)
                    gq.dma(g.dma_start(tdc[s_][:], td_d[sl, :]))
                    gq.dma(g.dma_start(cwc[s_][:], cw_d[sl, :]))
                for p in range(16):
                    gv.memset(xln[p][:, 0:1], 0.0)
                g.wait_ge(gq.h, gq.n)

        with nc.Block() as blk:
            @blk.scalar
            def _(sc):
                for s_ in range(8):
                    sc.activation(acol[s_][:], tdc[s_][:], AF.Exp)
        with nc.Block() as blk:
            @blk.vector
            def _(v):
                v = VW(v, ve)
                for s_ in range(8):
                    v.tensor_single_scalar(acol[s_][:], acol[s_][:], -1.0,
                                           ALU.mult)
        with nc.Block() as blk:
            @blk.scalar
            def _(sc):
                for s_ in range(8):
                    sc.activation(acol[s_][:], acol[s_][:], AF.Exp)

        emit_ln_to_fm(nc, x1_d, xln, T, ident, sq, pe, ve, fm_col0=1,
                      inner=True, ln_w=lnw, ln_b=lnb)

        with nc.Block() as blk:
            @blk.gpsimd
            def _(g):
                for p in range(16):
                    gq.dma(g.dma_start(xln_out[p * 128:(p + 1) * 128, :],
                                       xln[p][:, 0:T + 1]))

        kres = [sb(f"kres_{m}", [128, T], BF16) for m in range(8)]
        vres = [sb(f"vres_{m}", [128, T], BF16) for m in range(8)]

        with ExitStack() as c2:
            sb2 = lambda *a: c2.enter_context(nc.sbuf_tensor(*a))
            wkT = [sb2(f"wkT_{k}", [128, 1024], BF16) for k in range(16)]
            wvT = [sb2(f"wvT_{k}", [128, 1024], BF16) for k in range(16)]
            xkb = [sb2(f"xkb_{i}", [128, 256], BF16) for i in range(2)]
            xvb = [sb2(f"xvb_{i}", [128, 256], BF16) for i in range(2)]
            dbb = [sb2(f"dbb_{i}", [128, 256], BF16) for i in range(2)]
            ps = [c2.enter_context(nc.psum_tensor(f"kv_ps{i}", [128, 512], F32))
                  for i in range(8)]
            with nc.Block() as blk:
                @blk.sync
                def _(s):
                    for kp in range(16):
                        sq.dma(s.dma_start_transpose(
                            wkT[kp][:], wk_bf[:, kp * 128:(kp + 1) * 128]))
                    for kp in range(16):
                        sq.dma(s.dma_start_transpose(
                            wvT[kp][:], wv_bf[:, kp * 128:(kp + 1) * 128]))
                    s.wait_ge(sq.h, sq.n)

            for tc in range(8):
                with nc.Block() as blk:
                    pe0, ve0 = pe.n, ve.n

                    @blk.tensor
                    def _(t):
                        for kp in range(16):
                            t.wait_ge(ve.h, ve0 + kp * 3 + 3)
                            first = kp == 0
                            last = kp == 15
                            for m in range(8):
                                bank, colr = m % 4, (m // 4) * 256
                                t.matmul(ps[bank][:, colr:colr + 256],
                                         wkT[kp][:, m * 128:(m + 1) * 128],
                                         xkb[kp % 2][:],
                                         start=(first and m < 4), stop=False)
                            for m in range(8):
                                bank, colr = 4 + m % 4, (m // 4) * 256
                                mm = t.matmul(ps[bank][:, colr:colr + 256],
                                              wvT[kp][:, m * 128:(m + 1) * 128],
                                              xvb[kp % 2][:],
                                              start=(first and m < 4),
                                              stop=(last and m == 7))
                            pe.inc(mm)

                    @blk.vector
                    def _(v):
                        v = VW(v, ve)
                        for kp in range(16):
                            if kp >= 2:
                                v.wait_ge(pe.h, pe0 + kp - 1)
                            xs = xln[kp][:, 1 + tc * 256: 1 + (tc + 1) * 256]
                            sx = xln[kp][:, tc * 256: (tc + 1) * 256]
                            v.tensor_sub(dbb[kp % 2][:], xs, sx)
                            v.scalar_tensor_tensor(xkb[kp % 2][:], dbb[kp % 2][:],
                                                   mk[kp][:], sx,
                                                   ALU.mult, ALU.add)
                            v.scalar_tensor_tensor(xvb[kp % 2][:], dbb[kp % 2][:],
                                                   mv[kp][:], sx,
                                                   ALU.mult, ALU.add)
                        v.wait_ge(pe.h, pe0 + 16)
                        for m in range(8):
                            bank, colr = m % 4, (m // 4) * 256
                            v.tensor_copy(kres[m][:, tc * 256:(tc + 1) * 256],
                                          ps[bank][:, colr:colr + 256])
                        for m in range(8):
                            bank, colr = 4 + m % 4, (m // 4) * 256
                            v.tensor_copy(vres[m][:, tc * 256:(tc + 1) * 256],
                                          ps[bank][:, colr:colr + 256])

        with ExitStack() as c2:
            sb2 = lambda *a: c2.enter_context(nc.sbuf_tensor(*a))
            bcast = sb2("bcast", [128, T], F32)
            ek = sb2("ek", [128, T], BF16)
            ekw = sb2("ekw", [128, T], BF16)
            gg = sb2("gg", [128, T], BF16)
            hh = sb2("hh", [128, T], BF16)
            numb = sb2("numb", [128, T + 1], F32)
            denb = sb2("denb", [128, T + 1], F32)
            nm = sb2("nm", [128, T], F32)
            dn = sb2("dn", [128, T], F32)
            wkvt = sb2("wkvt", [128, T], BF16)
            for s_ in range(8):
                with nc.Block() as blk:
                    @blk.scalar
                    def _(sc):
                        sc.activation(ek[:], kres[s_][:], AF.Exp)
                        sc.activation(ekw[:], kres[s_][:], AF.Exp,
                                      bias=cwc[s_][:])
                with nc.Block() as blk:
                    @blk.vector
                    def _(v):
                        v = VW(v, ve)
                        v.memset(bcast[:], 1.0)
                        v.tensor_single_scalar(bcast[:], bcast[:], acol[s_][:],
                                               ALU.mult)
                        v.tensor_mul(gg[:], ek[:], vres[s_][:])
                        v.tensor_mul(hh[:], ekw[:], vres[s_][:])
                        v.memset(numb[:, 0:1], 0.0)
                        v.memset(denb[:, 0:1], 0.0)
                        v.tensor_tensor_scan(numb[:, 1:T + 1], bcast[:], gg[:],
                                             0.0, ALU.mult, ALU.add)
                        v.tensor_tensor_scan(denb[:, 1:T + 1], bcast[:], ek[:],
                                             0.0, ALU.mult, ALU.add)
                        v.tensor_add(nm[:], numb[:, 0:T], hh[:])
                        v.tensor_add(dn[:], denb[:, 0:T], ekw[:])
                        v.reciprocal(dn[:], dn[:])
                        v.tensor_mul(wkvt[:], nm[:], dn[:])
                with nc.Block() as blk:
                    @blk.gpsimd
                    def _(g):
                        gq.dma(g.dma_start(
                            wkv_out[s_ * 128:(s_ + 1) * 128, :], wkvt[:]))
                        g.wait_ge(gq.h, gq.n)

        with nc.Block() as blk:
            @blk.gpsimd
            def _(g):
                g.wait_ge(gq.h, gq.n)
    return nc


def _in_maps_b(inputs, a2_res):
    col = lambda v: np.ascontiguousarray(np.asarray(v).reshape(-1, 1))
    ident = np.eye(128, dtype=np.float32).astype(ml_dtypes.bfloat16)
    wk = np.asarray(inputs["L1_attn_Wk"])
    wv = np.asarray(inputs["L1_attn_Wv"])
    td = np.asarray(inputs["L1_attn_time_decay"])
    cw = np.asarray(inputs["L1_attn_current_token_weight"])
    base = {
        "lnw": col(inputs["L1_attn_ln_w"]),
        "lnb": col(inputs["L1_attn_ln_b"]),
        "mixk": col(inputs["L1_attn_mix_k"]),
        "mixv": col(inputs["L1_attn_mix_v"]),
        "ident": ident,
    }
    in_maps = []
    for c in range(8):
        b, half = c // 2, c % 2
        rows = slice(half * 1024, half * 1024 + 1024)
        x1_full = np.concatenate([np.asarray(a2_res[2 * b]["x1_out"]),
                                  np.asarray(a2_res[2 * b + 1]["x1_out"])],
                                 axis=0)
        m = dict(base)
        m["x1f"] = np.ascontiguousarray(x1_full)
        m["wk1"] = np.ascontiguousarray(wk[rows])
        m["wv1"] = np.ascontiguousarray(wv[rows])
        m["tdec"] = col(td[rows])
        m["ctw"] = col(cw[rows])
        in_maps.append(m)
    return in_maps


def run_b(inputs, a2_res):
    nc = build_b()
    res = run_bass_kernel_spmd(nc, _in_maps_b(inputs, a2_res),
                               core_ids=list(range(8)))
    return res.results


def build_c(F=None):
    nc = F if F is not None else bass.Bass("TRN2", target_bir_lowering=False)
    precast = getattr(F, "precast", False)
    x1_d = nc.dram_tensor("x1own", [HTOK, D], F32, kind="ExternalInput")
    xlnE_d = nc.dram_tensor("xlnE", [D, HTOK + 1], BF16, kind="ExternalInput")
    wkvo_d = nc.dram_tensor("wkvo", [D, HTOK], BF16, kind="ExternalInput")
    wr_d = nc.dram_tensor("wr1", [D, D], F32, kind="ExternalInput")
    wo_d = nc.dram_tensor("wo1", [D, D], F32, kind="ExternalInput")
    mr_d = nc.dram_tensor("mixr", [D, 1], F32, kind="ExternalInput")
    gw_d = nc.dram_tensor("gate_w1", [D, 8], F32, kind="ExternalInput")
    wgu_d = nc.dram_tensor("wgu", [8, D, 2048], F32, kind="ExternalInput")
    wdn_d = nc.dram_tensor("wdn", [8, 1024, D], F32, kind="ExternalInput")
    swu_d = nc.dram_tensor("swu", [D, 8192], F32, kind="ExternalInput")
    swd_d = nc.dram_tensor("swd", [4096, D], F32, kind="ExternalInput")
    ident_d = nc.dram_tensor("ident", [128, 128], BF16, kind="ExternalInput")

    out_d = nc.dram_tensor("out_own", [HTOK, D], F32, kind="ExternalOutput")

    wr_bf = nc.dram_tensor("wr_bf", [D, D], BF16, kind="Internal")
    wo_bf = nc.dram_tensor("wo_bf", [D, D], BF16, kind="Internal")
    xn1tm_d = nc.dram_tensor("xn1tm_d", [HTOK, D], BF16, kind="Internal")
    s1_d = nc.dram_tensor("s1_d", [HTOK, D], F32, kind="Internal")

    with ExitStack() as ctx:
        sb = lambda *a: ctx.enter_context(nc.sbuf_tensor(*a))
        gq = Sem(ctx, nc, "gq")
        sq = Sem(ctx, nc, "sq")
        pe = Sem(ctx, nc, "pe")
        ve = Sem(ctx, nc, "ve")
        se = Sem(ctx, nc, "se")

        xn1 = [sb(f"xn1_{p}", [128, HTOK], BF16) for p in range(16)]
        gates = [sb(f"gates_{t}", [128, 8], F32) for t in range(8)]

        if not precast:
            with nc.Block() as blk:
                @blk.gpsimd
                def _(g):
                    gq.dma(g.dma_start(wr_bf[:], wr_d[:]))
                    gq.dma(g.dma_start(wo_bf[:], wo_d[:]))
                    g.wait_ge(gq.h, gq.n)

        with ExitStack() as c0:
            ident = c0.enter_context(nc.sbuf_tensor("identb", [128, 128], BF16))
            with nc.Block() as blk:
                @blk.gpsimd
                def _(g):
                    gq.dma(g.dma_start(ident[:], ident_d[:]))
                    g.wait_ge(gq.h, gq.n)
            ktm = [c0.enter_context(nc.sbuf_tensor(f"ktm_{t}", [128, D], BF16))
                   for t in range(8)]
            emit_ln_to_fm(nc, x1_d, xn1, HTOK, ident, sq, pe, ve, keep_tm=ktm)
            with nc.Block() as blk:
                @blk.gpsimd
                def _(g):
                    for t in range(8):
                        gq.dma(g.dma_start(xn1tm_d[t * 128:(t + 1) * 128, :],
                                           ktm[t][:]))
                    g.wait_ge(gq.h, gq.n)

        with ExitStack() as cA:
            sbA = lambda *a: cA.enter_context(nc.sbuf_tensor(*a))
            wkvo = [sbA(f"wkvo_{p}", [128, HTOK], BF16) for p in range(16)]
            rr = [sbA(f"rr_{p}", [128, HTOK], BF16) for p in range(16)]
            with nc.Block() as blk:
                @blk.gpsimd
                def _(g):
                    for p in range(16):
                        gq.dma(g.dma_start(wkvo[p][:],
                                           wkvo_d[p * 128:(p + 1) * 128, :]))
                    g.wait_ge(gq.h, gq.n)

            with ExitStack() as cB:
                sbB = lambda *a: cB.enter_context(nc.sbuf_tensor(*a))
                xlnE = [sbB(f"xlnE_{p}", [128, HTOK + 1], BF16)
                        for p in range(16)]
                mr = [sbB(f"mr_{p}", [128, 1], F32) for p in range(16)]
                wrT = [sbB(f"wrT_{k}", [128, D], BF16) for k in range(16)]
                xrb = [sbB(f"xrb_{i}", [128, 256], BF16) for i in range(2)]
                drb = [sbB(f"drb_{i}", [128, 256], BF16) for i in range(2)]
                ps = [cB.enter_context(
                    nc.psum_tensor(f"r_ps{i}", [128, 512], F32))
                    for i in range(8)]
                with nc.Block() as blk:
                    @blk.gpsimd
                    def _(g):
                        for p in range(16):
                            gq.dma(g.dma_start(
                                xlnE[p][:], xlnE_d[p * 128:(p + 1) * 128, :]))
                            gq.dma(g.dma_start(
                                mr[p][:], mr_d[p * 128:(p + 1) * 128, :]))
                        g.wait_ge(gq.h, gq.n)

                    @blk.sync
                    def _(s):
                        for kp in range(16):
                            sq.dma(s.dma_start_transpose(
                                wrT[kp][:], wr_bf[:, kp * 128:(kp + 1) * 128]))
                        s.wait_ge(sq.h, sq.n)

                for tc in range(4):
                    with nc.Block() as blk:
                        pe0, ve0 = pe.n, ve.n

                        @blk.tensor
                        def _(t):
                            for kp in range(16):
                                t.wait_ge(ve.h, ve0 + kp * 2 + 2)
                                first, last = kp == 0, kp == 15
                                for m in range(16):
                                    bank, colr = m % 8, (m // 8) * 256
                                    mm = t.matmul(
                                        ps[bank][:, colr:colr + 256],
                                        wrT[kp][:, m * 128:(m + 1) * 128],
                                        xrb[kp % 2][:],
                                        start=(first and m < 8),
                                        stop=(last and m == 15))
                                pe.inc(mm)

                        @blk.vector
                        def _(v):
                            v = VW(v, ve)
                            for kp in range(16):
                                if kp >= 2:
                                    v.wait_ge(pe.h, pe0 + kp - 1)
                                xs = xlnE[kp][:, 1 + tc * 256:
                                              1 + (tc + 1) * 256]
                                sx = xlnE[kp][:, tc * 256:(tc + 1) * 256]
                                v.tensor_sub(drb[kp % 2][:], xs, sx)
                                v.scalar_tensor_tensor(
                                    xrb[kp % 2][:], drb[kp % 2][:],
                                    mr[kp][:], sx, ALU.mult, ALU.add)
                            v.wait_ge(pe.h, pe0 + 16)
                            for m in range(16):
                                bank, colr = m % 8, (m // 8) * 256
                                v.tensor_copy(
                                    rr[m][:, tc * 256:(tc + 1) * 256],
                                    ps[bank][:, colr:colr + 256])

            with nc.Block() as blk:
                @blk.scalar
                def _(sc):
                    for p in range(16):
                        sc.activation(rr[p][:], rr[p][:], AF.Sigmoid)
            with nc.Block() as blk:
                @blk.vector
                def _(v):
                    v = VW(v, ve)
                    for p in range(16):
                        v.tensor_mul(rr[p][:], rr[p][:], wkvo[p][:])

            with ExitStack() as cC:
                sbC = lambda *a: cC.enter_context(nc.sbuf_tensor(*a))
                woT = [sbC(f"woT_{k}", [128, D], BF16) for k in range(16)]
                s1t = [sbC(f"s1t_{i}", [128, D], F32) for i in range(2)]
                ps = [cC.enter_context(
                    nc.psum_tensor(f"wo_ps{i}", [128, 512], F32))
                    for i in range(8)]
                with nc.Block() as blk:
                    @blk.sync
                    def _(s):
                        for kp in range(16):
                            sq.dma(s.dma_start_transpose(
                                woT[kp][:], wo_bf[:, kp * 128:(kp + 1) * 128]))
                        s.wait_ge(sq.h, sq.n)
                with nc.Block() as blk:
                    pe0, ve0, gq0 = pe.n, ve.n, gq.n

                    @blk.tensor
                    def _(t):
                        for tt in range(8):
                            bs = (tt % 2) * 4
                            if tt >= 2:
                                t.wait_ge(ve.h, ve0 + (tt - 2) * 4 + 4)
                            for kp in range(16):
                                for oc in range(4):
                                    mm = t.matmul(
                                        ps[bs + oc][:],
                                        rr[kp][:, tt * 128:(tt + 1) * 128],
                                        woT[kp][:, oc * 512:(oc + 1) * 512],
                                        start=(kp == 0), stop=(kp == 15))
                            pe.inc(mm)

                    @blk.vector
                    def _(v):
                        v = VW(v, ve)
                        for tt in range(8):
                            bs = (tt % 2) * 4
                            v.wait_ge(pe.h, pe0 + tt + 1)
                            if tt >= 2:
                                v.wait_ge(gq.h, gq0 + (tt - 1) * 16)
                            for oc in range(4):
                                v.tensor_copy(
                                    s1t[tt % 2][:, oc * 512:(oc + 1) * 512],
                                    ps[bs + oc][:])

                    @blk.gpsimd
                    def _(g):
                        for tt in range(8):
                            g.wait_ge(ve.h, ve0 + tt * 4 + 4)
                            gq.dma(g.dma_start(
                                s1_d[tt * 128:(tt + 1) * 128, :],
                                s1t[tt % 2][:]))
                            g.wait_ge(gq.h, gq.n)

        acc = [sb(f"acc_{t}", [128, D], F32) for t in range(8)]
        with nc.Block() as blk:
            @blk.gpsimd
            def _(g):
                gv = VW(g, gq)
                for t in range(8):
                    gv.memset(acc[t][:], 0.0)

        with ExitStack() as cG:
            psg = [cG.enter_context(nc.psum_tensor("g_ps", [128, 64], F32))]
            gwb = [cG.enter_context(nc.sbuf_tensor(f"gwb_{p}", [128, 8], BF16))
                   for p in range(16)]
            lg = [cG.enter_context(nc.sbuf_tensor(f"lg{t}", [128, 8], F32))
                  for t in range(8)]
            m1 = [cG.enter_context(nc.sbuf_tensor(f"m1_{t}", [128, 1], F32))
                  for t in range(8)]
            nm1 = [cG.enter_context(nc.sbuf_tensor(f"nm1_{t}", [128, 1], F32))
                   for t in range(8)]
            ee = [cG.enter_context(nc.sbuf_tensor(f"ee{t}", [128, 8], F32))
                  for t in range(8)]
            scr = [cG.enter_context(nc.sbuf_tensor(f"sc{t}", [128, 8], F32))
                   for t in range(8)]
            zz = [cG.enter_context(nc.sbuf_tensor(f"zz{t}", [128, 1], F32))
                  for t in range(8)]
            with nc.Block() as blk:
                @blk.gpsimd
                def _(g):
                    for p in range(16):
                        gq.dma(g.dma_start(gwb[p][:],
                                           gw_d[p * 128:(p + 1) * 128, :]))
                    g.wait_ge(gq.h, gq.n)
            with nc.Block() as blk:
                pe0, ve0 = pe.n, ve.n

                @blk.tensor
                def _(t):
                    for tt in range(8):
                        for kp in range(16):
                            mm = t.matmul(psg[0][:, tt * 8:(tt + 1) * 8],
                                          xn1[kp][:, tt * 128:(tt + 1) * 128],
                                          gwb[kp][:],
                                          start=(tt == 0 and kp == 0),
                                          stop=(tt == 7 and kp == 15))
                    pe.inc(mm)

                @blk.vector
                def _(v):
                    v = VW(v, ve)
                    v.wait_ge(pe.h, pe0 + 1)
                    for tt in range(8):
                        v.tensor_copy(lg[tt][:], psg[0][:, tt * 8:(tt + 1) * 8])
                        v.reduce_max(m1[tt][:], lg[tt][:], axis=AX)
                        v.tensor_single_scalar(nm1[tt][:], m1[tt][:], -1.0,
                                               ALU.mult)
            with nc.Block() as blk:
                @blk.scalar
                def _(s):
                    for tt in range(8):
                        s.activation(ee[tt][:], lg[tt][:], AF.Exp,
                                     bias=nm1[tt][:])
            with nc.Block() as blk:
                @blk.vector
                def _(v):
                    v = VW(v, ve)
                    for tt in range(8):
                        v.tensor_single_scalar(scr[tt][:], lg[tt][:], m1[tt][:],
                                               ALU.is_ge)
                        v.scalar_tensor_tensor(scr[tt][:], scr[tt][:], -1e9,
                                               lg[tt][:], ALU.mult, ALU.add)
                        v.reduce_max(zz[tt][:], scr[tt][:], axis=AX)
                        v.tensor_single_scalar(scr[tt][:], lg[tt][:], zz[tt][:],
                                               ALU.is_ge)
                        v.tensor_mul(scr[tt][:], scr[tt][:], ee[tt][:])
                        v.reduce_sum(zz[tt][:], scr[tt][:], axis=AX)
                        v.reciprocal(zz[tt][:], zz[tt][:])
                        v.tensor_single_scalar(gates[tt][:], scr[tt][:],
                                               zz[tt][:], ALU.mult)

        with ExitStack() as cF:
            sbF = lambda *a: cF.enter_context(nc.sbuf_tensor(*a))
            wbig = [sbF(f"wbig_{k}", [128, 2048], BF16) for k in range(16)]
            actf = [sbF(f"actf_{m}", [128, HTOK], BF16) for m in range(8)]
            actu = [sbF(f"actu_{m}", [128, HTOK], BF16) for m in range(8)]
            ps = [cF.enter_context(
                nc.psum_tensor(f"ff_ps{i}", [128, 512], F32))
                for i in range(8)]
            with nc.Block() as blk:
                @blk.gpsimd
                def _(g):
                    for kp in range(16):
                        gq.dma(g.dma_start(
                            wbig[kp][:], wgu_d[0, kp * 128:(kp + 1) * 128, :]))
                    g.wait_ge(gq.h, gq.n)
            for e in range(8):
                with nc.Block() as blk:
                    pe0, ve0 = pe.n, ve.n
                    jobs = [(m, tc) for m in range(16) for tc in range(2)]

                    @blk.tensor
                    def _(t):
                        for ji, (m, tc) in enumerate(jobs):
                            bank = ji % 8
                            if ji >= 8:
                                t.wait_ge(ve.h, ve0 + ji - 7)
                            for kp in range(16):
                                mm = t.matmul(
                                    ps[bank][:],
                                    wbig[kp][:, m * 128:(m + 1) * 128],
                                    xn1[kp][:, tc * 512:(tc + 1) * 512],
                                    start=(kp == 0), stop=(kp == 15))
                            pe.inc(mm)

                    @blk.vector
                    def _(v):
                        v = VW(v, ve)
                        for ji, (m, tc) in enumerate(jobs):
                            bank = ji % 8
                            v.wait_ge(pe.h, pe0 + ji + 1)
                            dst = actf[m] if m < 8 else actu[m - 8]
                            v.tensor_copy(dst[:, tc * 512:(tc + 1) * 512],
                                          ps[bank][:])
                with nc.Block() as blk:
                    @blk.scalar
                    def _(sc):
                        for m in range(8):
                            sc.activation(actf[m][:], actf[m][:], AF.Silu)

                    @blk.gpsimd
                    def _(g):
                        for m in range(8):
                            gq.dma(g.dma_start(
                                wbig[m][:], wdn_d[e, m * 128:(m + 1) * 128, :]))
                        g.wait_ge(gq.h, gq.n)
                with nc.Block() as blk:
                    @blk.vector
                    def _(v):
                        v = VW(v, ve)
                        for m in range(8):
                            v.tensor_mul(actf[m][:], actf[m][:], actu[m][:])
                with nc.Block() as blk:
                    pe0, ve0 = pe.n, ve.n
                    jobs = [(tt, oc) for tt in range(8) for oc in range(4)]

                    @blk.gpsimd
                    def _(g):
                        for kp in range(8, 16):
                            if e + 1 < 8:
                                gq.dma(g.dma_start(
                                    wbig[kp][:],
                                    wgu_d[e + 1, kp * 128:(kp + 1) * 128, :]))
                            else:
                                gq.dma(g.dma_start(
                                    wbig[kp][:, 0:1024],
                                    swu_d[kp * 128:(kp + 1) * 128, 0:1024]))
                                gq.dma(g.dma_start(
                                    wbig[kp][:, 1024:2048],
                                    swu_d[kp * 128:(kp + 1) * 128, 4096:5120]))
                        g.wait_ge(gq.h, gq.n)

                    @blk.tensor
                    def _(t):
                        for ji, (tt, oc) in enumerate(jobs):
                            bank = ji % 8
                            if ji >= 8:
                                t.wait_ge(ve.h, ve0 + ji - 7)
                            for m in range(8):
                                mm = t.matmul(
                                    ps[bank][:],
                                    actf[m][:, tt * 128:(tt + 1) * 128],
                                    wbig[m][:, oc * 512:(oc + 1) * 512],
                                    start=(m == 0), stop=(m == 7))
                            pe.inc(mm)

                    @blk.vector
                    def _(v):
                        v = VW(v, ve)
                        for ji, (tt, oc) in enumerate(jobs):
                            bank = ji % 8
                            v.wait_ge(pe.h, pe0 + ji + 1)
                            v.scalar_tensor_tensor(
                                acc[tt][:, oc * 512:(oc + 1) * 512],
                                ps[bank][:], gates[tt][:, e:e + 1],
                                acc[tt][:, oc * 512:(oc + 1) * 512],
                                ALU.mult, ALU.add)
                with nc.Block() as blk:
                    @blk.gpsimd
                    def _(g):
                        for kp in range(0, 8):
                            if e + 1 < 8:
                                gq.dma(g.dma_start(
                                    wbig[kp][:],
                                    wgu_d[e + 1, kp * 128:(kp + 1) * 128, :]))
                            else:
                                gq.dma(g.dma_start(
                                    wbig[kp][:, 0:1024],
                                    swu_d[kp * 128:(kp + 1) * 128, 0:1024]))
                                gq.dma(g.dma_start(
                                    wbig[kp][:, 1024:2048],
                                    swu_d[kp * 128:(kp + 1) * 128, 4096:5120]))
                        g.wait_ge(gq.h, gq.n)

            for hc in range(4):
                with nc.Block() as blk:
                    pe0, ve0 = pe.n, ve.n
                    jobs = [(m, tc) for m in range(16) for tc in range(2)]

                    @blk.tensor
                    def _(t):
                        for ji, (m, tc) in enumerate(jobs):
                            bank = ji % 8
                            if ji >= 8:
                                t.wait_ge(ve.h, ve0 + ji - 7)
                            colw = (m % 8) * 128 + (m // 8) * 1024
                            for kp in range(16):
                                mm = t.matmul(
                                    ps[bank][:],
                                    wbig[kp][:, colw:colw + 128],
                                    xn1[kp][:, tc * 512:(tc + 1) * 512],
                                    start=(kp == 0), stop=(kp == 15))
                            pe.inc(mm)

                    @blk.vector
                    def _(v):
                        v = VW(v, ve)
                        for ji, (m, tc) in enumerate(jobs):
                            bank = ji % 8
                            v.wait_ge(pe.h, pe0 + ji + 1)
                            dst = actf[m] if m < 8 else actu[m - 8]
                            v.tensor_copy(dst[:, tc * 512:(tc + 1) * 512],
                                          ps[bank][:])
                with nc.Block() as blk:
                    @blk.scalar
                    def _(sc):
                        for m in range(8):
                            sc.activation(actf[m][:], actf[m][:], AF.Silu)

                    @blk.gpsimd
                    def _(g):
                        for m in range(8):
                            gq.dma(g.dma_start(
                                wbig[m][:],
                                swd_d[hc * 1024 + m * 128:
                                      hc * 1024 + (m + 1) * 128, :]))
                        g.wait_ge(gq.h, gq.n)
                with nc.Block() as blk:
                    @blk.vector
                    def _(v):
                        v = VW(v, ve)
                        for m in range(8):
                            v.tensor_mul(actf[m][:], actf[m][:], actu[m][:])
                with nc.Block() as blk:
                    pe0, ve0 = pe.n, ve.n
                    jobs = [(tt, oc) for tt in range(8) for oc in range(4)]

                    @blk.gpsimd
                    def _(g):
                        if hc + 1 < 4:
                            nxt = hc + 1
                            for kp in range(8, 16):
                                gq.dma(g.dma_start(
                                    wbig[kp][:, 0:1024],
                                    swu_d[kp * 128:(kp + 1) * 128,
                                          nxt * 1024:(nxt + 1) * 1024]))
                                gq.dma(g.dma_start(
                                    wbig[kp][:, 1024:2048],
                                    swu_d[kp * 128:(kp + 1) * 128,
                                          4096 + nxt * 1024:
                                          4096 + (nxt + 1) * 1024]))
                            g.wait_ge(gq.h, gq.n)

                    @blk.tensor
                    def _(t):
                        for ji, (tt, oc) in enumerate(jobs):
                            bank = ji % 8
                            if ji >= 8:
                                t.wait_ge(ve.h, ve0 + ji - 7)
                            for m in range(8):
                                mm = t.matmul(
                                    ps[bank][:],
                                    actf[m][:, tt * 128:(tt + 1) * 128],
                                    wbig[m][:, oc * 512:(oc + 1) * 512],
                                    start=(m == 0), stop=(m == 7))
                            pe.inc(mm)

                    @blk.vector
                    def _(v):
                        v = VW(v, ve)
                        for ji, (tt, oc) in enumerate(jobs):
                            bank = ji % 8
                            v.wait_ge(pe.h, pe0 + ji + 1)
                            v.tensor_add(acc[tt][:, oc * 512:(oc + 1) * 512],
                                         acc[tt][:, oc * 512:(oc + 1) * 512],
                                         ps[bank][:])
                if hc + 1 < 4:
                    with nc.Block() as blk:
                        @blk.gpsimd
                        def _(g):
                            nxt = hc + 1
                            for kp in range(0, 8):
                                gq.dma(g.dma_start(
                                    wbig[kp][:, 0:1024],
                                    swu_d[kp * 128:(kp + 1) * 128,
                                          nxt * 1024:(nxt + 1) * 1024]))
                                gq.dma(g.dma_start(
                                    wbig[kp][:, 1024:2048],
                                    swu_d[kp * 128:(kp + 1) * 128,
                                          4096 + nxt * 1024:
                                          4096 + (nxt + 1) * 1024]))
                            g.wait_ge(gq.h, gq.n)

        with ExitStack() as c5:
            g1t = c5.enter_context(nc.sbuf_tensor("g1t", [128, D], F32))
            xnb = c5.enter_context(nc.sbuf_tensor("xnb5", [128, D], BF16))
            rwk = c5.enter_context(nc.sbuf_tensor("rwk", [128, D], F32))
            x1t = c5.enter_context(nc.sbuf_tensor("x1t5", [128, D], F32))
            ot = c5.enter_context(nc.sbuf_tensor("ot", [128, D], F32))
            for tt in range(8):
                with nc.Block() as blk:
                    @blk.sync
                    def _(s):
                        sl = slice(tt * 128, (tt + 1) * 128)
                        sq.dma(s.dma_start(xnb[:], xn1tm_d[sl, :]))
                        sq.dma(s.dma_start(rwk[:], s1_d[sl, :]))
                        sq.dma(s.dma_start(x1t[:], x1_d[sl, :]))
                        s.wait_ge(sq.h, sq.n)

                    @blk.scalar
                    def _(sc):
                        sc.activation(g1t[:], acc[tt][:], AF.Silu)
                with nc.Block() as blk:
                    @blk.vector
                    def _(v):
                        v = VW(v, ve)
                        v.tensor_add(rwk[:], rwk[:], xnb[:])
                        v.tensor_mul(rwk[:], rwk[:], g1t[:])
                        v.tensor_add(ot[:], x1t[:], rwk[:])
                with nc.Block() as blk:
                    @blk.gpsimd
                    def _(g):
                        gq.dma(g.dma_start(out_d[tt * 128:(tt + 1) * 128, :],
                                           ot[:]))
                        g.wait_ge(gq.h, gq.n)
    return nc


def _in_maps_c(inputs, b_res):
    col = lambda v: np.ascontiguousarray(np.asarray(v).reshape(-1, 1))
    ident = np.eye(128, dtype=np.float32).astype(ml_dtypes.bfloat16)
    base = {
        "wr1": np.asarray(inputs["L1_attn_Wr"]),
        "wo1": np.asarray(inputs["L1_attn_Wo"]),
        "mixr": col(inputs["L1_attn_mix_r"]),
        "gate_w1": np.asarray(inputs["L1_ffn_w_gate"]),
        "wgu": np.asarray(inputs["L1_ffn_w_gate_up"]),
        "wdn": np.asarray(inputs["L1_ffn_w_down"]),
        "swu": np.asarray(inputs["L1_ffn_w_sh_gate_up"]),
        "swd": np.asarray(inputs["L1_ffn_w_sh_down"]),
        "ident": ident,
    }
    in_maps = []
    for c in range(8):
        b, half = c // 2, c % 2
        cols = slice(half * 1024, half * 1024 + 1024)
        xln = np.asarray(b_res[c]["xln_out"])
        T0 = half * 1024
        xlnE = xln[:, T0:T0 + 1025]
        wkvo = np.concatenate([np.asarray(b_res[2 * b]["wkv_out"])[:, cols],
                               np.asarray(b_res[2 * b + 1]["wkv_out"])[:, cols]],
                              axis=0)
        m = dict(base)
        m["xlnE"] = np.ascontiguousarray(xlnE)
        m["wkvo"] = np.ascontiguousarray(wkvo)
        in_maps.append(m)
    return in_maps


def run_c(inputs, a2_res, b_res):
    nc = build_c()
    in_maps = _in_maps_c(inputs, b_res)
    for c in range(8):
        in_maps[c]["x1own"] = np.ascontiguousarray(
            np.asarray(a2_res[c]["x1_out"]))
    res = run_bass_kernel_spmd(nc, in_maps, core_ids=list(range(8)))
    return res.results


def kernel_v0(**inputs):
    a1 = run_a1(inputs)
    a2 = run_a2(inputs, a1)
    b_ = run_b(inputs, a2)
    cc = run_c(inputs, a2, b_)
    out = np.empty((B, T, D), dtype=np.float32)
    for c in range(8):
        b, half = c // 2, c % 2
        out[b, half * 1024:(half + 1) * 1024] = np.asarray(cc[c]["out_own"])
    return out


import os as _os
_os.environ.setdefault("TRN_ALLOW_ICI_ALL_TO_ALL", "1")
from concourse.aot_env import allow_ici_all_to_all


class NCP:

    def __init__(self, nc, pfx, remap):
        self._nc = nc
        self._p = pfx
        self._remap = remap

    def set_pfx(self, pfx):
        self._p = pfx

    def sbuf_tensor(self, name, *a, **k):
        return self._nc.sbuf_tensor(self._p + name, *a, **k)

    def psum_tensor(self, name, *a, **k):
        return self._nc.psum_tensor(self._p + name, *a, **k)

    def semaphore(self, name=None, num=None):
        return self._nc.semaphore(self._p + (name or "s"), num)

    def dram_tensor(self, name, shape, dt, kind="Internal"):
        if name in self._remap:
            return self._remap[name]
        return self._nc.dram_tensor(self._p + name, shape, dt, kind="Internal")

    def Block(self, *a, **k):
        return self._nc.Block(*a, **k)

    def __getattr__(self, n):
        return getattr(self._nc, n)


def build_fused():
    nc = bass.Bass("TRN2", target_bir_lowering=False)
    PAIRS = [[0, 1], [2, 3], [4, 5], [6, 7]]
    ext_in = {}

    def ein(name, shape, dt):
        ext_in[name] = nc.dram_tensor(name, shape, dt, kind="ExternalInput")
        return ext_in[name]

    ein("x0", [T, D], F32)
    ein("wq_p", [1024, D], F32)
    ein("wk_h", [256, D], F32)
    ein("wv_h", [256, D], F32)
    ein("ident", [128, 128], BF16)
    ein("stair", [128, 896], F32)
    ein("x0own", [HTOK, D], F32)
    ein("wo", [D, D], F32)
    ein("gate_w", [D, 8], F32)
    ein("w1", [8, D, 1024], F32)
    ein("b1c", [8192, 1], F32)
    ein("w2", [8, 1024, D], F32)
    ein("b2ext", [1, 9 * D], F32)
    ein("sw1", [D, D], F32)
    ein("sb1c", [D, 1], F32)
    ein("sw2", [D, D], F32)
    ein("wk1", [1024, D], F32)
    ein("wv1", [1024, D], F32)
    ein("lnw", [D, 1], F32)
    ein("lnb", [D, 1], F32)
    ein("mixk", [D, 1], F32)
    ein("mixv", [D, 1], F32)
    ein("tdec", [1024, 1], F32)
    ein("ctw", [1024, 1], F32)
    ein("wr1", [D, D], F32)
    ein("wo1", [D, D], F32)
    ein("mixr", [D, 1], F32)
    ein("gate_w1", [D, 8], F32)
    ein("wgu", [8, D, 2048], F32)
    ein("wdn", [8, 1024, D], F32)
    ein("swu", [D, 8192], F32)
    ein("swd", [4096, D], F32)

    out_d = nc.dram_tensor("out_own", [HTOK, D], F32, kind="ExternalOutput")

    xn0_d = nc.dram_tensor("f_xn0", [D, T], BF16, kind="Internal")
    attn_d = nc.dram_tensor("f_attn", [1024, T], BF16, kind="Internal")
    attn_g = nc.dram_tensor("f_attng", [2048, T], BF16, kind="Internal")
    xln_d = nc.dram_tensor("f_xln", [D, T + 1], BF16, kind="Internal")
    wkv_d = nc.dram_tensor("f_wkv", [1024, T], BF16, kind="Internal")
    x1own_d = nc.dram_tensor("f_x1own", [HTOK, D], F32, kind="Internal")
    x1g_o = nc.dram_tensor("f_x1go", [T, D], F32, kind="Internal")
    wkv_g = nc.dram_tensor("f_wkvg", [2048, T], BF16, kind="Internal")
    xn0own_d = nc.dram_tensor("f_xn0own", [D, 1024], BF16, kind="Internal")
    attno_d = nc.dram_tensor("f_attno", [2048, 1024], BF16, kind="Internal")
    xlnx_d = nc.dram_tensor("f_xlnx", [D, 1025], BF16, kind="Internal")
    wkvo_d = nc.dram_tensor("f_wkvo", [2048, 1024], BF16, kind="Internal")

    t0i = nc.dram_tensor("t0i", [1, 1], mybir.dt.int32, kind="ExternalInput")
    ext_in["t0i"] = t0i

    wo_bf_a2 = nc.dram_tensor("f_wobf2", [D, D], BF16, kind="Internal")
    wk_bf_b = nc.dram_tensor("f_wkbf", [1024, D], BF16, kind="Internal")
    wv_bf_b = nc.dram_tensor("f_wvbf", [1024, D], BF16, kind="Internal")
    wr_bf_c = nc.dram_tensor("f_wrbf", [D, D], BF16, kind="Internal")
    wo_bf_c = nc.dram_tensor("f_wobf1", [D, D], BF16, kind="Internal")

    ccs = nc.semaphore("f_cc").__enter__()
    gqs = nc.semaphore("f_gq").__enter__()
    t0s = nc.sbuf_tensor("f_t0s", [1, 1], mybir.dt.int32).__enter__()
    rt0 = nc.gpsimd.register("f_rt0").__enter__()
    ccn = [0]
    gqn = [0]
    ld0 = [False]

    def cc_block(ccs_list, sel_list):
        with nc.Block() as blk:
            @blk.gpsimd
            def _(g):
                if not ld0[0]:
                    g.dma_start(t0s[:], t0i[:]).then_inc(gqs, 16)
                    gqn[0] += 16
                    g.wait_ge(gqs, gqn[0])
                    g.reg_load(rt0, t0s[0:1, 0:1])
                    ld0[0] = True
                with allow_ici_all_to_all():
                    for iap, oap in ccs_list:
                        g.collective_compute(
                            "AllGather", ALU.bypass, replica_groups=PAIRS,
                            ins=[iap], outs=[oap]).then_inc(ccs, 1)
                        ccn[0] += 1
                g.wait_ge(ccs, ccn[0])
                for dst_t, src_t, nrows, rstride, ncols in sel_list:
                    g.dma_start(
                        bass.AP(dst_t, 0, [[ncols, nrows], [1, ncols]]),
                        bass.AP(src_t, rt0, [[rstride, nrows], [1, ncols]]),
                    ).then_inc(gqs, 16)
                    gqn[0] += 16
                g.wait_ge(gqs, gqn[0])

    with nc.Block() as blk:
        @blk.gpsimd
        def _(g):
            for dst, src in [(wo_bf_a2, ext_in["wo"]),
                             (wk_bf_b, ext_in["wk1"]), (wv_bf_b, ext_in["wv1"]),
                             (wr_bf_c, ext_in["wr1"]), (wo_bf_c, ext_in["wo1"])]:
                g.dma_start(dst.ap(), src.ap()).then_inc(gqs, 16)
                gqn[0] += 16

    remap = {"x0": ext_in["x0"], "wq_p": ext_in["wq_p"],
             "wk_h": ext_in["wk_h"], "wv_h": ext_in["wv_h"],
             "ident": ext_in["ident"], "stair": ext_in["stair"],
             "attn_out": attn_d, "xn0_out": xn0_d}
    build_a1(F=NCP(nc, "A_", remap))

    cc_block([(attn_d.ap(), attn_g.ap())],
             [(xn0own_d, xn0_d, D, T, 1024),
              (attno_d, attn_g, 2048, T, 1024)])

    remap = {"xn0f": xn0own_d, "attnf": attno_d, "x0own": ext_in["x0own"],
             "wo": ext_in["wo"], "gate_w": ext_in["gate_w"],
             "w1": ext_in["w1"], "b1c": ext_in["b1c"], "w2": ext_in["w2"],
             "b2ext": ext_in["b2ext"], "sw1": ext_in["sw1"],
             "sb1c": ext_in["sb1c"], "sw2": ext_in["sw2"],
             "x1_out": x1own_d, "wo_bf": wo_bf_a2}
    f = NCP(nc, "B_", remap)
    f.precast = True
    build_a2(F=f)

    cc_block([(x1own_d.ap(), x1g_o.ap())], [])

    remap = {"x1f": x1g_o, "wk1": ext_in["wk1"], "wv1": ext_in["wv1"],
             "lnw": ext_in["lnw"], "lnb": ext_in["lnb"],
             "mixk": ext_in["mixk"], "mixv": ext_in["mixv"],
             "tdec": ext_in["tdec"], "ctw": ext_in["ctw"],
             "ident": ext_in["ident"],
             "xln_out": xln_d, "wkv_out": wkv_d,
             "wk_bf": wk_bf_b, "wv_bf": wv_bf_b}
    f = NCP(nc, "C_", remap)
    f.precast = True
    build_b(F=f)

    cc_block([(wkv_d.ap(), wkv_g.ap())],
             [(xlnx_d, xln_d, D, T + 1, 1025),
              (wkvo_d, wkv_g, 2048, T, 1024)])

    remap = {"x1own": x1own_d, "xlnE": xlnx_d, "wkvo": wkvo_d,
             "wr1": ext_in["wr1"], "wo1": ext_in["wo1"],
             "mixr": ext_in["mixr"], "gate_w1": ext_in["gate_w1"],
             "wgu": ext_in["wgu"], "wdn": ext_in["wdn"],
             "swu": ext_in["swu"], "swd": ext_in["swd"],
             "ident": ext_in["ident"], "out_own": out_d,
             "wr_bf": wr_bf_c, "wo_bf": wo_bf_c}
    f = NCP(nc, "D_", remap)
    f.precast = True
    build_c(F=f)
    return nc


def kernel_fused(**inputs):
    nc = build_fused()
    m_a1 = _in_maps_a1(inputs)
    col = lambda v: np.ascontiguousarray(np.asarray(v).reshape(-1, 1))
    x = np.asarray(inputs["L0_attn_x"])
    wk = np.asarray(inputs["L1_attn_Wk"])
    wv = np.asarray(inputs["L1_attn_Wv"])
    td = np.asarray(inputs["L1_attn_time_decay"])
    cw = np.asarray(inputs["L1_attn_current_token_weight"])
    base = {
        "wo": np.asarray(inputs["L0_attn_wo"]),
        "gate_w": np.asarray(inputs["L0_ffn_gate_w"]),
        "w1": np.asarray(inputs["L0_ffn_w1"]),
        "b1c": np.ascontiguousarray(
            np.asarray(inputs["L0_ffn_b1"]).reshape(-1, 1)),
        "w2": np.asarray(inputs["L0_ffn_w2"]),
        "b2ext": np.ascontiguousarray(np.concatenate(
            [np.asarray(inputs["L0_ffn_b2"]).reshape(-1),
             np.asarray(inputs["L0_ffn_shared_b2"]).reshape(-1)])[None, :]),
        "sw1": np.asarray(inputs["L0_ffn_shared_w1"]),
        "sb1c": col(inputs["L0_ffn_shared_b1"]),
        "sw2": np.asarray(inputs["L0_ffn_shared_w2"]),
        "lnw": col(inputs["L1_attn_ln_w"]),
        "lnb": col(inputs["L1_attn_ln_b"]),
        "mixk": col(inputs["L1_attn_mix_k"]),
        "mixv": col(inputs["L1_attn_mix_v"]),
        "mixr": col(inputs["L1_attn_mix_r"]),
        "wr1": np.asarray(inputs["L1_attn_Wr"]),
        "wo1": np.asarray(inputs["L1_attn_Wo"]),
        "gate_w1": np.asarray(inputs["L1_ffn_w_gate"]),
        "wgu": np.asarray(inputs["L1_ffn_w_gate_up"]),
        "wdn": np.asarray(inputs["L1_ffn_w_down"]),
        "swu": np.asarray(inputs["L1_ffn_w_sh_gate_up"]),
        "swd": np.asarray(inputs["L1_ffn_w_sh_down"]),
    }
    in_maps = []
    for c in range(8):
        b, half = c // 2, c % 2
        rows = slice(half * 1024, half * 1024 + 1024)
        m = dict(base)
        m.update(m_a1[c])
        m["x0own"] = np.ascontiguousarray(x[b, rows])
        m["t0i"] = np.array([[half * 1024]], dtype=np.int32)
        m["wk1"] = np.ascontiguousarray(wk[rows])
        m["wv1"] = np.ascontiguousarray(wv[rows])
        m["tdec"] = col(td[rows])
        m["ctw"] = col(cw[rows])
        in_maps.append(m)
    res = run_bass_kernel_spmd(nc, in_maps, core_ids=list(range(8)))
    out = np.empty((B, T, D), dtype=np.float32)
    for c in range(8):
        b, half = c // 2, c % 2
        out[b, half * 1024:(half + 1) * 1024] = np.asarray(
            res.results[c]["out_own"])
    return out


# version 10
# speedup vs baseline: 1.0772x; 1.0772x over previous
import sys
if "/opt/trn_rl_repo" not in sys.path:
    sys.path.insert(0, "/opt/trn_rl_repo")
from contextlib import ExitStack

import numpy as np
import ml_dtypes

import concourse.bass as bass
import concourse.mybir as mybir
from concourse.bass_utils import run_bass_kernel_spmd

F32 = mybir.dt.float32
BF16 = mybir.dt.bfloat16
AF = mybir.ActivationFunctionType
ALU = mybir.AluOpType
AX = mybir.AxisListType.X

B, T, D = 4, 2048, 2048
HTOK = 1024
HD = 64

QLO = [0, 1, 2, 3, 8, 9, 10, 11]
QHI = [4, 5, 6, 7, 12, 13, 14, 15]
QPERM = []
for _p in range(8):
    QPERM += list(range(QLO[_p] * HD, QLO[_p] * HD + HD))
    QPERM += list(range(QHI[_p] * HD, QHI[_p] * HD + HD))
SLOT = {}
for _p in range(8):
    SLOT[QLO[_p]] = (_p, 0)
    SLOT[QHI[_p]] = (_p, 64)


class Sem:
    def __init__(self, ctx, nc, name):
        self.h = ctx.enter_context(nc.semaphore(name))
        self.n = 0

    def dma(self, instr):
        instr.then_inc(self.h, 16)
        self.n += 16
        return self.n

    def inc(self, instr, k=1):
        instr.then_inc(self.h, k)
        self.n += k
        return self.n


_VOPS = {"tensor_copy", "tensor_tensor", "tensor_add", "tensor_sub",
         "tensor_mul", "tensor_max", "tensor_scalar", "tensor_single_scalar",
         "tensor_scalar_add", "tensor_scalar_sub", "tensor_scalar_mul",
         "tensor_scalar_max", "tensor_scalar_min", "tensor_relu",
         "tensor_reduce", "reduce_max", "reduce_sum", "scalar_tensor_tensor",
         "tensor_tensor_scan", "reciprocal", "memset", "memzero", "select",
         "copy_predicated", "transpose", "iota", "bn_stats", "bn_aggr"}


class VW:

    def __init__(self, eng, sem):
        self._e = eng
        self._s = sem

    def __getattr__(self, name):
        fn = getattr(self._e, name)
        if name not in _VOPS:
            return fn
        sem = self._s
        eng = self._e

        def wrapped(*a, **k):
            eng.wait_ge(sem.h, sem.n)
            ins = fn(*a, **k)
            sem.inc(ins)
            return ins
        return wrapped


def host_stair():
    u = np.arange(896)[None, :]
    p = np.arange(128)[:, None]
    return np.where(u < p + 384, -1e9, 0.0).astype(np.float32)


def emit_ln_to_fm(nc, x_dram, dst_fm, nrows, ident, sq, pe, ve, fm_col0=0,
                  inner=False, ln_w=None, ln_b=None, keep_tm=None, x_row0=0,
                  x_is_sbuf=False):
    G = 4
    ntiles = nrows // 128
    ngroups = ntiles // G
    with ExitStack() as c2:
        sb = lambda *a: c2.enter_context(nc.sbuf_tensor(*a))
        xt = [sb(f"ln_x{i}", [128, D], F32) for i in range(G)]
        xn2 = [sb(f"ln_n{i}", [128, D], F32) for i in range(G)] if inner else None
        sq_s = sb("ln_sq", [128, D], F32)
        xnb = [sb(f"ln_b{i}", [128, D], BF16) for i in range(G)]
        sums = [sb(f"ln_su{i}", [128, 1], F32) for i in range(G)]
        sumsq = [sb(f"ln_sq{i}", [128, 1], F32) for i in range(G)]
        mean = [sb(f"ln_mn{i}", [128, 1], F32) for i in range(G)]
        var = [sb(f"ln_va{i}", [128, 1], F32) for i in range(G)]
        rstd = [sb(f"ln_rs{i}", [128, 1], F32) for i in range(G)]
        psbf = [c2.enter_context(nc.psum_tensor(f"ln_ps{i}", [128, 512], BF16))
                for i in range(2)]

        for g in range(ngroups):
            r0 = g * G
            if not x_is_sbuf:
                with nc.Block() as blk:
                    @blk.sync
                    def _(s):
                        for i in range(G):
                            row = x_row0 + (r0 + i) * 128
                            sq.dma(s.dma_start(xt[i][:], x_dram[row:row + 128, :]))
                        s.wait_ge(sq.h, sq.n)
                srcs = xt
            else:
                srcs = [x_dram[r0 + i] for i in range(G)]
            passes = 2 if inner else 1
            for pss in range(passes):
                last = pss == passes - 1
                dsts = xnb if last else xn2
                with nc.Block() as blk:
                    @blk.vector
                    def _(v):
                        v = VW(v, ve)
                        for i in range(G):
                            v.reduce_sum(sums[i][:], srcs[i][:], axis=AX)

                    @blk.scalar
                    def _(s):
                        for i in range(G):
                            s.activation(sq_s[:], srcs[i][:], AF.Square,
                                         accum_out=sumsq[i][:])
                with nc.Block() as blk:
                    @blk.vector
                    def _(v):
                        v = VW(v, ve)
                        for i in range(G):
                            v.tensor_single_scalar(mean[i][:], sums[i][:],
                                                   1.0 / D, ALU.mult)
                            v.tensor_mul(var[i][:], mean[i][:], mean[i][:])
                            v.scalar_tensor_tensor(var[i][:], sumsq[i][:],
                                                   1.0 / D, var[i][:],
                                                   ALU.mult, ALU.subtract)
                            v.tensor_single_scalar(var[i][:], var[i][:],
                                                   1e-5, ALU.add)
                with nc.Block() as blk:
                    @blk.scalar
                    def _(s):
                        for i in range(G):
                            s.activation(var[i][:], var[i][:], AF.Ln)
                            s.activation(rstd[i][:], var[i][:], AF.Exp,
                                         scale=-0.5)
                with nc.Block() as blk:
                    @blk.vector
                    def _(v):
                        v = VW(v, ve)
                        for i in range(G):
                            v.tensor_scalar(dsts[i][:], srcs[i][:],
                                            mean[i][:], rstd[i][:],
                                            ALU.subtract, ALU.mult)
                srcs = xn2
            if keep_tm is not None:
                with nc.Block() as blk:
                    @blk.vector
                    def _(v):
                        v = VW(v, ve)
                        for i in range(G):
                            v.tensor_copy(keep_tm[r0 + i][:], xnb[i][:])
            with nc.Block() as blk:
                pe0, ve0 = pe.n, ve.n

                @blk.tensor
                def _(t):
                    for i in range(G):
                        for j in range(16):
                            jj = i * 16 + j
                            bank = jj % 2
                            slot = (jj // 2) % 4
                            if jj >= 2:
                                t.wait_ge(ve.h, ve0 + jj - 1)
                            pe.inc(t.transpose(
                                psbf[bank][:, slot * 128:(slot + 1) * 128],
                                xnb[i][:, j * 128:(j + 1) * 128], ident[:]))

                @blk.vector
                def _(v):
                    v = VW(v, ve)
                    for i in range(G):
                        for j in range(16):
                            jj = i * 16 + j
                            bank = jj % 2
                            slot = (jj // 2) % 4
                            v.wait_ge(pe.h, pe0 + jj + 1)
                            col = fm_col0 + (r0 + i) * 128
                            src = psbf[bank][:, slot * 128:(slot + 1) * 128]
                            if ln_w is not None:
                                v.tensor_scalar(dst_fm[j][:, col:col + 128], src,
                                                ln_w[j][:], ln_b[j][:],
                                                ALU.mult, ALU.add)
                            else:
                                v.tensor_copy(dst_fm[j][:, col:col + 128], src)


def build_a1(upto=3, nqc=4, nheads=16, parts=7, F=None):
    nc = F if F is not None else bass.Bass("TRN2", target_bir_lowering=False)
    x0 = nc.dram_tensor("x0", [T, D], F32, kind="ExternalInput")
    wq_p = nc.dram_tensor("wq_p", [1024, D], F32, kind="ExternalInput")
    wk_h = nc.dram_tensor("wk_h", [256, D], F32, kind="ExternalInput")
    wv_h = nc.dram_tensor("wv_h", [256, D], F32, kind="ExternalInput")
    ident_d = nc.dram_tensor("ident", [128, 128], BF16, kind="ExternalInput")
    stair_d = nc.dram_tensor("stair", [128, 896], F32, kind="ExternalInput")

    attn_out = nc.dram_tensor("attn_out", [1024, T], BF16, kind="ExternalOutput")
    xn0_out = nc.dram_tensor("xn0_out", [D, T], BF16, kind="ExternalOutput")

    wq_bf = nc.dram_tensor("wq_bf", [1024, D], BF16, kind="Internal")
    wk_bf = nc.dram_tensor("wk_bf", [256, D], BF16, kind="Internal")
    wv_bf = nc.dram_tensor("wv_bf", [256, D], BF16, kind="Internal")

    with ExitStack() as ctx:
        sb = lambda *a: ctx.enter_context(nc.sbuf_tensor(*a))
        gq = Sem(ctx, nc, "gq")
        sq = Sem(ctx, nc, "sq")
        pe = Sem(ctx, nc, "pe")
        ve = Sem(ctx, nc, "ve")
        se = Sem(ctx, nc, "se")

        xn0_fm = [sb(f"xn0_{p}", [128, T], BF16) for p in range(16)]
        ident = sb("identb", [128, 128], BF16)
        stair = sb("stairb", [128, 896], F32)

        with nc.Block() as blk:
            @blk.gpsimd
            def _(g):
                gq.dma(g.dma_start(wq_bf[:], wq_p[:]))
                gq.dma(g.dma_start(wk_bf[:], wk_h[:]))
                gq.dma(g.dma_start(wv_bf[:], wv_h[:]))
                gq.dma(g.dma_start(ident[:], ident_d[:]))
                gq.dma(g.dma_start(stair[:], stair_d[:]))
                g.wait_ge(gq.h, gq.n)

        emit_ln_to_fm(nc, x0, xn0_fm, T, ident, sq, pe, ve)

        with nc.Block() as blk:
            @blk.gpsimd
            def _(g):
                for p in range(16):
                    gq.dma(g.dma_start(xn0_out[p * 128:(p + 1) * 128, :],
                                       xn0_fm[p][:]))

        wqT = [sb(f"wqT_{k}", [128, 1024], BF16) for k in range(16)]
        wkT = [sb(f"wkT_{k}", [128, 256], BF16) for k in range(16)]
        wvT = [sb(f"wvT_{k}", [128, 256], BF16) for k in range(16)]
        kfm = [sb(f"kfm_{p}", [128, T], BF16) for p in range(2)]
        vbuf = [sb(f"vbuf_{k}", [128, 260], BF16) for k in range(16)]
        qfm = [sb(f"qfm_{p}", [128, 512], BF16) for p in range(8)]
        attn_fm = [sb(f"attn_{p}", [128, T], BF16) for p in range(8)]
        onesb = sb("onesb", [128, 128], F32)
        sraw = [sb(f"sraw{i}", [128, 16 * 512], BF16) for i in range(2)]
        rrow = sb("rrow", [128, 512], F32)
        araw = sb("araw", [128, 512], BF16)

        with nc.Block() as blk:
            @blk.sync
            def _(s):
                for kp in range(16):
                    sq.dma(s.dma_start_transpose(
                        wqT[kp][:], wq_bf[:, kp * 128:(kp + 1) * 128]))
                for kp in range(16):
                    sq.dma(s.dma_start_transpose(
                        wkT[kp][:], wk_bf[:, kp * 128:(kp + 1) * 128]))
                for kp in range(16):
                    sq.dma(s.dma_start_transpose(
                        wvT[kp][:], wv_bf[:, kp * 128:(kp + 1) * 128]))
                s.wait_ge(sq.h, sq.n)

            @blk.gpsimd
            def _(g):
                gv = VW(g, gq)
                gv.memset(onesb[:], 1.0)
                for kk in range(16):
                    gv.memset(vbuf[kk][:], 1.0)

        if upto < 2:
            return nc
        with ExitStack() as c2:
            ps = [c2.enter_context(nc.psum_tensor(f"kv_ps{i}", [128, 512], F32))
                  for i in range(4)]
            with nc.Block() as blk:
                pe0, ve0 = pe.n, ve.n
                jobs = [("k", m, tc) for m in range(2) for tc in range(4)]
                jobs += [("v", kt, None) for kt in range(16)]

                @blk.tensor
                def _(t):
                    for mi, (kind, a1, a2) in enumerate(jobs):
                        bank = mi % 4
                        if mi >= 4:
                            t.wait_ge(ve.h, ve0 + mi - 3)
                        if kind == "k":
                            m, tc = a1, a2
                            for kp in range(16):
                                mm = t.matmul(
                                    ps[bank][:],
                                    wkT[kp][:, m * 128:(m + 1) * 128],
                                    xn0_fm[kp][:, tc * 512:(tc + 1) * 512],
                                    start=(kp == 0), stop=(kp == 15))
                        else:
                            kt = a1
                            for kp in range(16):
                                mm = t.matmul(
                                    ps[bank][:, 0:256],
                                    xn0_fm[kp][:, kt * 128:(kt + 1) * 128],
                                    wvT[kp][:],
                                    start=(kp == 0), stop=(kp == 15))
                        pe.inc(mm)

                @blk.vector
                def _(v):
                    v = VW(v, ve)
                    for mi, (kind, a1, a2) in enumerate(jobs):
                        bank = mi % 4
                        v.wait_ge(pe.h, pe0 + mi + 1)
                        if kind == "k":
                            m, tc = a1, a2
                            v.tensor_copy(kfm[m][:, tc * 512:(tc + 1) * 512],
                                          ps[bank][:])
                        else:
                            kt = a1
                            dst = bass.AP(vbuf[kt], 0,
                                          [[260, 128], [65, 4], [1, 64]])
                            src = ps[bank].ap()[:, 0:256].rearrange(
                                "p (g c) -> p g c", g=4)
                            v.tensor_copy(dst, src)

        if upto < 3:
            return nc
        with ExitStack() as c2:
            ps = [c2.enter_context(nc.psum_tensor(f"at_ps{i}", [128, 512], F32))
                  for i in range(5)]
            for qc in range(nqc):
                KT = qc * 4 + 4
                diag0 = qc * 4
                with nc.Block() as blk:
                    pe0, ve0 = pe.n, ve.n

                    @blk.tensor
                    def _(t):
                        for m in range(8):
                            bank = m % 4
                            if m >= 4:
                                t.wait_ge(ve.h, ve0 + m - 3)
                            for kp in range(16):
                                mm = t.matmul(
                                    ps[bank][:],
                                    wqT[kp][:, m * 128:(m + 1) * 128],
                                    xn0_fm[kp][:, qc * 512:(qc + 1) * 512],
                                    start=(kp == 0), stop=(kp == 15))
                            pe.inc(mm)

                    @blk.vector
                    def _(v):
                        v = VW(v, ve)
                        for m in range(8):
                            bank = m % 4
                            v.wait_ge(pe.h, pe0 + m + 1)
                            v.tensor_copy(qfm[m][:], ps[bank][:])

                for h in range(nheads + 1):
                    if parts & 1 == 0:
                        continue
                    with nc.Block() as blk:
                        pe0, ve0 = pe.n, ve.n
                        if h < nheads:
                            kvh = h // 4
                            kv_pt, kv_base = kvh // 2, (kvh % 2) * 64
                            q_pt, q_base = SLOT[h]
                            dve_marks = []
                            nops = 0
                            for kt in range(KT):
                                nops += 2 if kt >= diag0 else 1
                                dve_marks.append(nops)

                            @blk.tensor
                            def _(t):
                                for kt in range(KT):
                                    if kt >= 2:
                                        t.wait_ge(ve.h, ve0 + dve_marks[kt - 2])
                                    pe.inc(t.matmul(
                                        ps[kt % 2][:],
                                        kfm[kv_pt][kv_base:kv_base + 64,
                                                   kt * 128:(kt + 1) * 128],
                                        qfm[q_pt][q_base:q_base + 64, :],
                                        start=True, stop=True))

                            @blk.vector
                            def _(v):
                                v = VW(v, ve)
                                for kt in range(KT):
                                    v.wait_ge(pe.h, pe0 + kt + 1)
                                    if kt >= diag0:
                                        j = kt - diag0
                                        v.tensor_add(
                                            ps[kt % 2][:], ps[kt % 2][:],
                                            stair[:, 384 - j * 128:
                                                  896 - j * 128])
                                    v.tensor_copy(
                                        sraw[h % 2][:, kt * 512:(kt + 1) * 512],
                                        ps[kt % 2][:])
                        if h >= 1:
                            @blk.scalar
                            def _(sc):
                                sc.activation(sraw[(h - 1) % 2][:, 0:KT * 512],
                                              sraw[(h - 1) % 2][:, 0:KT * 512],
                                              AF.Exp, scale=0.125)
                    if h >= 1:
                        hp = h - 1
                        kvh = hp // 4
                        av = 2 + (hp % 2)
                        with nc.Block() as blk:
                            pe0, ve0 = pe.n, ve.n

                            @blk.tensor
                            def _(t):
                                for kt in range(KT):
                                    mm = t.matmul(
                                        ps[av][0:65, :],
                                        vbuf[kt][:, kvh * 65: kvh * 65 + 65],
                                        sraw[hp % 2][:, kt * 512:(kt + 1) * 512],
                                        start=(kt == 0), stop=(kt == KT - 1))
                                pe.inc(mm)
                                t.wait_ge(ve.h, ve0 + 1)
                                pe.inc(t.matmul(ps[4][0:64, :],
                                                onesb[0:1, 0:64],
                                                rrow[0:1, :],
                                                start=True, stop=True))

                            @blk.vector
                            def _(v):
                                v = VW(v, ve)
                                v.wait_ge(pe.h, pe0 + 1)
                                v.reciprocal(rrow[0:1, :], ps[av][64:65, :])
                                v.tensor_copy(araw[0:64, :], ps[av][0:64, :])
                                v.wait_ge(pe.h, pe0 + 2)
                                v.tensor_mul(
                                    attn_fm[hp // 2][(hp % 2) * 64:
                                                     (hp % 2) * 64 + 64,
                                                     qc * 512:(qc + 1) * 512],
                                    araw[0:64, :], ps[4][0:64, :])

        with nc.Block() as blk:
            @blk.gpsimd
            def _(g):
                for p in range(8):
                    gq.dma(g.dma_start(attn_out[p * 128:(p + 1) * 128, :],
                                       attn_fm[p][:]))
                g.wait_ge(gq.h, gq.n)
    return nc



def _in_maps_a1(inputs):
    x = np.asarray(inputs["L0_attn_x"])
    wq = np.asarray(inputs["L0_attn_wq"])
    wk = np.asarray(inputs["L0_attn_wk"])
    wv = np.asarray(inputs["L0_attn_wv"])
    ident = np.eye(128, dtype=np.float32).astype(ml_dtypes.bfloat16)
    stair = host_stair()
    in_maps = []
    for c in range(8):
        b, half = c // 2, c % 2
        wq_h = wq[half * 1024:(half + 1) * 1024][QPERM]
        in_maps.append({
            "x0": np.ascontiguousarray(x[b]),
            "wq_p": np.ascontiguousarray(wq_h),
            "wk_h": np.ascontiguousarray(wk[half * 256:(half + 1) * 256]),
            "wv_h": np.ascontiguousarray(wv[half * 256:(half + 1) * 256]),
            "ident": ident,
            "stair": stair,
        })
    return in_maps


def run_a1(inputs, upto=3, nqc=4, nheads=16, parts=7):
    nc = build_a1(upto, nqc, nheads, parts)
    res = run_bass_kernel_spmd(nc, _in_maps_a1(inputs), core_ids=list(range(8)))
    return res.results


def kernel(**inputs):
    return kernel_fused(**inputs)


def build_a2(upto=6, nexp=8, parts=7, gmode=0, F=None):
    nc = F if F is not None else bass.Bass("TRN2", target_bir_lowering=False)
    precast = getattr(F, "precast", False)
    xn0f_d = nc.dram_tensor("xn0f", [D, HTOK], BF16, kind="ExternalInput")
    attn_d = nc.dram_tensor("attnf", [D, HTOK], BF16, kind="ExternalInput")
    x0_d = nc.dram_tensor("x0own", [HTOK, D], F32, kind="ExternalInput")
    wo_d = nc.dram_tensor("wo", [D, D], F32, kind="ExternalInput")
    gw_d = nc.dram_tensor("gate_w", [D, 8], F32, kind="ExternalInput")
    w1_d = nc.dram_tensor("w1", [8, D, 1024], F32, kind="ExternalInput")
    b1_d = nc.dram_tensor("b1c", [8192, 1], F32, kind="ExternalInput")
    w2_d = nc.dram_tensor("w2", [8, 1024, D], F32, kind="ExternalInput")
    b2e_d = nc.dram_tensor("b2ext", [1, 9 * D], F32, kind="ExternalInput")
    sw1_d = nc.dram_tensor("sw1", [D, D], F32, kind="ExternalInput")
    sb1c_d = nc.dram_tensor("sb1c", [D, 1], F32, kind="ExternalInput")
    sw2_d = nc.dram_tensor("sw2", [D, D], F32, kind="ExternalInput")


    x1_out = nc.dram_tensor("x1_out", [HTOK, D], F32, kind="ExternalOutput")

    wo_bf = nc.dram_tensor("wo_bf", [D, D], BF16, kind="Internal")
    s0_d = nc.dram_tensor("s0_d", [HTOK, D], BF16, kind="Internal")

    with ExitStack() as ctx:
        sb = lambda *a: ctx.enter_context(nc.sbuf_tensor(*a))
        gq = Sem(ctx, nc, "gq")
        sq = Sem(ctx, nc, "sq")
        pe = Sem(ctx, nc, "pe")
        ve = Sem(ctx, nc, "ve")
        se = Sem(ctx, nc, "se")

        xn0 = [sb(f"xn0_{p}", [128, HTOK], BF16) for p in range(16)]
        gates = [sb(f"gates_{t}", [128, 8], F32) for t in range(8)]
        gwb = [sb(f"gwb_{p}", [128, 8], BF16) for p in range(16)]
        sb1c = [sb(f"sb1c_{m}", [128, 1], F32) for m in range(16)]
        onesr = sb("onesr", [1, HTOK], BF16)

        with nc.Block() as blk:
            @blk.gpsimd
            def _(g):
                gv = VW(g, gq)
                for p in range(16):
                    gq.dma(g.dma_start(xn0[p][:], xn0f_d[p * 128:(p + 1) * 128, :]))
                if not precast:
                    gq.dma(g.dma_start(wo_bf[:], wo_d[:]))
                for p in range(16):
                    gq.dma(g.dma_start(gwb[p][:], gw_d[p * 128:(p + 1) * 128, :]))
                for m in range(16):
                    gq.dma(g.dma_start(sb1c[m][:], sb1c_d[m * 128:(m + 1) * 128, :]))
                gv.memset(onesr[:], 1.0)
                g.wait_ge(gq.h, gq.n)

        with ExitStack() as c2:
            woT = [c2.enter_context(nc.sbuf_tensor(f"woT_{k}", [128, D], BF16))
                   for k in range(16)]
            attn = [c2.enter_context(nc.sbuf_tensor(f"at_{k}", [128, HTOK], BF16))
                    for k in range(16)]
            s0t = [c2.enter_context(nc.sbuf_tensor(f"s0_{t}", [128, D], BF16))
                   for t in range(2)]
            ps = [c2.enter_context(nc.psum_tensor(f"wo_ps{i}", [128, 512], F32))
                  for i in range(8)]
            with nc.Block() as blk:
                @blk.sync
                def _(s):
                    for kp in range(16):
                        sq.dma(s.dma_start_transpose(
                            woT[kp][:], wo_bf[:, kp * 128:(kp + 1) * 128]))
                    s.wait_ge(sq.h, sq.n)

                @blk.gpsimd
                def _(g):
                    for kp in range(16):
                        gq.dma(g.dma_start(attn[kp][:],
                                           attn_d[kp * 128:(kp + 1) * 128, :]))
                    g.wait_ge(gq.h, gq.n)

            with nc.Block() as blk:
                pe0, ve0, gq0 = pe.n, ve.n, gq.n

                @blk.tensor
                def _(t):
                    for tt in range(8):
                        bs = (tt % 2) * 4
                        if tt >= 2:
                            t.wait_ge(ve.h, ve0 + (tt - 2) * 4 + 4)
                        for kp in range(16):
                            for oc in range(4):
                                mm = t.matmul(
                                    ps[bs + oc][:],
                                    attn[kp][:, tt * 128:(tt + 1) * 128],
                                    woT[kp][:, oc * 512:(oc + 1) * 512],
                                    start=(kp == 0), stop=(kp == 15))
                        pe.inc(mm)

                @blk.vector
                def _(v):
                    v = VW(v, ve)
                    for tt in range(8):
                        bs = (tt % 2) * 4
                        v.wait_ge(pe.h, pe0 + tt + 1)
                        if tt >= 2:
                            v.wait_ge(gq.h, gq0 + (tt - 1) * 16)
                        for oc in range(4):
                            v.tensor_copy(s0t[tt % 2][:, oc * 512:(oc + 1) * 512],
                                          ps[bs + oc][:])

                @blk.gpsimd
                def _(g):
                    for tt in range(8):
                        g.wait_ge(ve.h, ve0 + tt * 4 + 4)
                        gq.dma(g.dma_start(s0_d[tt * 128:(tt + 1) * 128, :],
                                           s0t[tt % 2][:]))
                        g.wait_ge(gq.h, gq.n)

        acc = [sb(f"acc_{t}", [128, D], F32) for t in range(8)]
        with nc.Block() as blk:
            @blk.gpsimd
            def _(g):
                gv = VW(g, gq)
                for t in range(8):
                    gv.memset(acc[t][:], 0.0)

        if upto < 2:
            return nc
        with ExitStack() as c2:
            ps = [c2.enter_context(nc.psum_tensor("g_ps", [128, 64], F32))]
            lg = [c2.enter_context(nc.sbuf_tensor(f"lg{t}", [128, 8], F32))
                  for t in range(8)]
            m1 = [c2.enter_context(nc.sbuf_tensor(f"m1_{t}", [128, 1], F32))
                  for t in range(8)]
            nm1 = [c2.enter_context(nc.sbuf_tensor(f"nm1_{t}", [128, 1], F32))
                   for t in range(8)]
            ee = [c2.enter_context(nc.sbuf_tensor(f"ee{t}", [128, 8], F32))
                  for t in range(8)]
            scr = [c2.enter_context(nc.sbuf_tensor(f"sc{t}", [128, 8], F32))
                   for t in range(8)]
            zz = [c2.enter_context(nc.sbuf_tensor(f"zz{t}", [128, 1], F32))
                  for t in range(8)]
            with nc.Block() as blk:
                pe0, ve0 = pe.n, ve.n

                @blk.tensor
                def _(t):
                    for tt in range(8):
                        for kp in range(16):
                            mm = t.matmul(ps[0][:, tt * 8:(tt + 1) * 8],
                                          xn0[kp][:, tt * 128:(tt + 1) * 128],
                                          gwb[kp][:],
                                          start=(tt == 0 and kp == 0),
                                          stop=(tt == 7 and kp == 15))
                    pe.inc(mm)

                @blk.vector
                def _(v):
                    v = VW(v, ve)
                    v.wait_ge(pe.h, pe0 + 1)
                    for tt in range(8):
                        v.tensor_copy(lg[tt][:], ps[0][:, tt * 8:(tt + 1) * 8])
                        v.reduce_max(m1[tt][:], lg[tt][:], axis=AX)
                        v.tensor_single_scalar(nm1[tt][:], m1[tt][:], -1.0,
                                               ALU.mult)
            with nc.Block() as blk:
                @blk.scalar
                def _(s):
                    for tt in range(8):
                        s.activation(ee[tt][:], lg[tt][:], AF.Exp,
                                     bias=nm1[tt][:])
            with nc.Block() as blk:
                pe0, ve0, se0 = pe.n, ve.n, se.n

                @blk.vector
                def _(v):
                    v = VW(v, ve)
                    for tt in range(8):
                        v.tensor_single_scalar(scr[tt][:], lg[tt][:], m1[tt][:],
                                               ALU.is_ge)
                        v.scalar_tensor_tensor(scr[tt][:], scr[tt][:], -1e9,
                                               lg[tt][:], ALU.mult, ALU.add)
                        v.reduce_max(zz[tt][:], scr[tt][:], axis=AX)
                        v.tensor_single_scalar(scr[tt][:], lg[tt][:], zz[tt][:],
                                               ALU.is_ge)
                        v.tensor_mul(scr[tt][:], scr[tt][:], ee[tt][:])
                        v.reduce_sum(zz[tt][:], scr[tt][:], axis=AX)
                        v.reciprocal(zz[tt][:], zz[tt][:])
                        v.tensor_single_scalar(gates[tt][:], scr[tt][:],
                                               zz[tt][:], ALU.mult)

        if upto < 3:
            return nc
        with ExitStack() as c2:
            w1s = [c2.enter_context(nc.sbuf_tensor(f"w1s_{k}", [128, 1024], BF16))
                   for k in range(16)]
            w2s = [c2.enter_context(nc.sbuf_tensor(f"w2s_{m}", [128, D], BF16))
                   for m in range(8)]
            act2 = [[c2.enter_context(nc.sbuf_tensor(f"act_{i}_{m}",
                                                     [128, HTOK], BF16))
                     for m in range(8)] for i in range(2)]
            b1c2 = [[c2.enter_context(nc.sbuf_tensor(f"b1cc_{i}_{m}",
                                                     [128, 1], F32))
                     for m in range(8)] for i in range(2)]
            b2t2 = [c2.enter_context(nc.sbuf_tensor(f"b2tt_{i}", [1, D], BF16))
                    for i in range(2)]
            ps = [c2.enter_context(nc.psum_tensor(f"ff_ps{i}", [128, 512], F32))
                  for i in range(8)]
            with nc.Block() as blk:
                @blk.gpsimd
                def _(g):
                    for kp in range(16):
                        gq.dma(g.dma_start(
                            w1s[kp][:], w1_d[0, kp * 128:(kp + 1) * 128, :]))
                    g.wait_ge(gq.h, gq.n)
            for e in range(nexp + 1):
                if e < nexp:
                    with nc.Block() as blk:
                        pe0, ve0 = pe.n, ve.n
                        jobs = [(m, tc) for m in range(8) for tc in range(2)]

                        @blk.gpsimd
                        def _(g):
                            if e >= 1:
                                for m in range(8):
                                    gq.dma(g.dma_start(
                                        w2s[m][:],
                                        w2_d[e - 1, m * 128:(m + 1) * 128, :]))
                            for m in range(8):
                                gq.dma(g.dma_start(
                                    b1c2[e % 2][m][:],
                                    b1_d[e * 1024 + m * 128:
                                         e * 1024 + (m + 1) * 128, :]))
                            gq.dma(g.dma_start(b2t2[e % 2][:],
                                               b2e_d[0:1, e * D:(e + 1) * D]))
                            g.wait_ge(gq.h, gq.n)

                        @blk.tensor
                        def _(t):
                            for ji, (m, tc) in enumerate(jobs):
                                bank = ji % 8
                                if ji >= 8:
                                    t.wait_ge(ve.h, ve0 + ji - 7)
                                for kp in range(16):
                                    mm = t.matmul(
                                        ps[bank][:],
                                        w1s[kp][:, m * 128:(m + 1) * 128],
                                        xn0[kp][:, tc * 512:(tc + 1) * 512],
                                        start=(kp == 0), stop=(kp == 15))
                                pe.inc(mm)

                        @blk.vector
                        def _(v):
                            vv = VW(v, ve)
                            for ji, (m, tc) in enumerate(jobs):
                                bank = ji % 8
                                v.wait_ge(pe.h, pe0 + ji + 1)
                                vv.tensor_copy(
                                    act2[e % 2][m][:, tc * 512:(tc + 1) * 512],
                                    ps[bank][:])

                        if e >= 1:
                            @blk.scalar
                            def _(sc):
                                for m in range(8):
                                    sc.activation(act2[(e - 1) % 2][m][:],
                                                  act2[(e - 1) % 2][m][:],
                                                  AF.Gelu,
                                                  bias=b1c2[(e - 1) % 2][m][:])
                else:
                    with nc.Block() as blk:
                        @blk.gpsimd
                        def _(g):
                            for m in range(8):
                                gq.dma(g.dma_start(
                                    w2s[m][:],
                                    w2_d[e - 1, m * 128:(m + 1) * 128, :]))
                            g.wait_ge(gq.h, gq.n)

                        @blk.scalar
                        def _(sc):
                            for m in range(8):
                                sc.activation(act2[(e - 1) % 2][m][:],
                                              act2[(e - 1) % 2][m][:],
                                              AF.Gelu,
                                              bias=b1c2[(e - 1) % 2][m][:])
                if e >= 1:
                    ep = e - 1
                    with nc.Block() as blk:
                        pe0, ve0 = pe.n, ve.n
                        jobs = [(tt, oc) for tt in range(8) for oc in range(4)]

                        @blk.gpsimd
                        def _(g):
                            if e + 1 < nexp + 1 and e < nexp:
                                for kp in range(16):
                                    gq.dma(g.dma_start(
                                        w1s[kp][:],
                                        w1_d[e, kp * 128:(kp + 1) * 128, :]))
                                g.wait_ge(gq.h, gq.n)

                        @blk.tensor
                        def _(t):
                            for ji, (tt, oc) in enumerate(jobs):
                                bank = ji % 8
                                if ji >= 8:
                                    t.wait_ge(ve.h, ve0 + ji - 7)
                                for m in range(8):
                                    t.matmul(
                                        ps[bank][:],
                                        act2[ep % 2][m][:, tt * 128:(tt + 1) * 128],
                                        w2s[m][:, oc * 512:(oc + 1) * 512],
                                        start=(m == 0), stop=False)
                                mm = t.matmul(
                                    ps[bank][:],
                                    onesr[0:1, tt * 128:(tt + 1) * 128],
                                    b2t2[ep % 2][0:1, oc * 512:(oc + 1) * 512],
                                    start=False, stop=True)
                                pe.inc(mm)

                        @blk.vector
                        def _(v):
                            vv = VW(v, ve)
                            for ji, (tt, oc) in enumerate(jobs):
                                bank = ji % 8
                                v.wait_ge(pe.h, pe0 + ji + 1)
                                vv.scalar_tensor_tensor(
                                    acc[tt][:, oc * 512:(oc + 1) * 512],
                                    ps[bank][:], gates[tt][:, ep:ep + 1],
                                    acc[tt][:, oc * 512:(oc + 1) * 512],
                                    ALU.mult, ALU.add)

        with ExitStack() as c2:
            sws = [c2.enter_context(nc.sbuf_tensor(f"sws_{k}", [128, D], BF16))
                   for k in range(16)]
            b2t = c2.enter_context(nc.sbuf_tensor("b2t", [1, D], BF16))
            acs = [c2.enter_context(nc.sbuf_tensor(f"acs_{m}", [128, HTOK], BF16))
                   for m in range(16)]
            ps = [c2.enter_context(nc.psum_tensor(f"sh_ps{i}", [128, 512], F32))
                  for i in range(8)]
            with nc.Block() as blk:
                @blk.gpsimd
                def _(g):
                    for kp in range(16):
                        gq.dma(g.dma_start(sws[kp][:],
                                           sw1_d[kp * 128:(kp + 1) * 128, :]))
                    g.wait_ge(gq.h, gq.n)
            with nc.Block() as blk:
                pe0, ve0 = pe.n, ve.n
                jobs = [(m, tc) for m in range(16) for tc in range(2)]

                @blk.tensor
                def _(t):
                    for ji, (m, tc) in enumerate(jobs):
                        bank = ji % 8
                        if ji >= 8:
                            t.wait_ge(ve.h, ve0 + ji - 7)
                        for kp in range(16):
                            mm = t.matmul(ps[bank][:],
                                          sws[kp][:, m * 128:(m + 1) * 128],
                                          xn0[kp][:, tc * 512:(tc + 1) * 512],
                                          start=(kp == 0), stop=(kp == 15))
                        pe.inc(mm)

                @blk.vector
                def _(v):
                    vv = VW(v, ve)
                    for ji, (m, tc) in enumerate(jobs):
                        bank = ji % 8
                        v.wait_ge(pe.h, pe0 + ji + 1)
                        vv.tensor_copy(acs[m][:, tc * 512:(tc + 1) * 512],
                                       ps[bank][:])
            with nc.Block() as blk:
                @blk.scalar
                def _(sc):
                    for m in range(16):
                        sc.activation(acs[m][:], acs[m][:], AF.Gelu,
                                      bias=sb1c[m][:])
            with nc.Block() as blk:
                @blk.gpsimd
                def _(g):
                    for kp in range(16):
                        gq.dma(g.dma_start(sws[kp][:],
                                           sw2_d[kp * 128:(kp + 1) * 128, :]))
                    gq.dma(g.dma_start(b2t[:], b2e_d[0:1, 8 * D:9 * D]))
                    g.wait_ge(gq.h, gq.n)
            with nc.Block() as blk:
                pe0, ve0 = pe.n, ve.n
                jobs = [(tt, oc) for tt in range(8) for oc in range(4)]

                @blk.tensor
                def _(t):
                    for ji, (tt, oc) in enumerate(jobs):
                        bank = ji % 8
                        if ji >= 8:
                            t.wait_ge(ve.h, ve0 + ji - 7)
                        for m in range(16):
                            t.matmul(ps[bank][:],
                                     acs[m][:, tt * 128:(tt + 1) * 128],
                                     sws[m][:, oc * 512:(oc + 1) * 512],
                                     start=(m == 0), stop=False)
                        mm = t.matmul(
                            ps[bank][:],
                            onesr[0:1, tt * 128:(tt + 1) * 128],
                            b2t[0:1, oc * 512:(oc + 1) * 512],
                            start=False, stop=True)
                        pe.inc(mm)

                @blk.vector
                def _(v):
                    v = VW(v, ve)
                    for ji, (tt, oc) in enumerate(jobs):
                        bank = ji % 8
                        v.wait_ge(pe.h, pe0 + ji + 1)
                        v.tensor_add(acc[tt][:, oc * 512:(oc + 1) * 512],
                                     acc[tt][:, oc * 512:(oc + 1) * 512],
                                     ps[bank][:])

        if upto < 6:
            return nc
        with ExitStack() as c2:
            g0t = c2.enter_context(nc.sbuf_tensor("g0t", [128, D], F32))
            s0b = c2.enter_context(nc.sbuf_tensor("s0b", [128, D], BF16))
            x0t = c2.enter_context(nc.sbuf_tensor("x0t", [128, D], F32))
            x1t = c2.enter_context(nc.sbuf_tensor("x1t", [128, D], F32))
            for tt in range(8):
                with nc.Block() as blk:
                    @blk.sync
                    def _(s):
                        sq.dma(s.dma_start(s0b[:], s0_d[tt * 128:(tt + 1) * 128, :]))
                        sq.dma(s.dma_start(x0t[:], x0_d[tt * 128:(tt + 1) * 128, :]))
                        s.wait_ge(sq.h, sq.n)

                    @blk.scalar
                    def _(s):
                        s.activation(g0t[:], acc[tt][:], AF.Silu)
                with nc.Block() as blk:
                    @blk.vector
                    def _(v):
                        v = VW(v, ve)
                        v.tensor_mul(g0t[:], g0t[:], s0b[:])
                        v.tensor_add(x1t[:], x0t[:], g0t[:])
                with nc.Block() as blk:
                    @blk.gpsimd
                    def _(g):
                        gq.dma(g.dma_start(x1_out[tt * 128:(tt + 1) * 128, :],
                                           x1t[:]))
                        g.wait_ge(gq.h, gq.n)
    return nc


def _in_maps_a2(inputs, a1_res):
    x = np.asarray(inputs["L0_attn_x"])
    ident = np.eye(128, dtype=np.float32).astype(ml_dtypes.bfloat16)
    base = {
        "wo": np.asarray(inputs["L0_attn_wo"]),
        "gate_w": np.asarray(inputs["L0_ffn_gate_w"]),
        "w1": np.asarray(inputs["L0_ffn_w1"]),
        "b1c": np.ascontiguousarray(np.asarray(inputs["L0_ffn_b1"]).reshape(-1, 1)),
        "w2": np.asarray(inputs["L0_ffn_w2"]),
        "b2ext": np.ascontiguousarray(np.concatenate(
            [np.asarray(inputs["L0_ffn_b2"]).reshape(-1),
             np.asarray(inputs["L0_ffn_shared_b2"]).reshape(-1)])[None, :]),
        "sw1": np.asarray(inputs["L0_ffn_shared_w1"]),
        "sb1c": np.ascontiguousarray(
            np.asarray(inputs["L0_ffn_shared_b1"])[:, None]),
        "sw2": np.asarray(inputs["L0_ffn_shared_w2"]),
    }
    in_maps = []
    for c in range(8):
        b, half = c // 2, c % 2
        cols = slice(half * 1024, half * 1024 + 1024)
        attn_full = np.concatenate(
            [np.asarray(a1_res[2 * b]["attn_out"])[:, cols],
             np.asarray(a1_res[2 * b + 1]["attn_out"])[:, cols]], axis=0)
        m = dict(base)
        m["xn0f"] = np.ascontiguousarray(
            np.asarray(a1_res[2 * b + half]["xn0_out"])[:, cols])
        m["attnf"] = np.ascontiguousarray(attn_full)
        m["x0own"] = np.ascontiguousarray(x[b, half * 1024:half * 1024 + 1024])
        in_maps.append(m)
    return in_maps


def run_a2(inputs, a1_res, upto=6, nexp=8, parts=7, gmode=0):
    nc = build_a2(upto, nexp, parts, gmode)
    res = run_bass_kernel_spmd(nc, _in_maps_a2(inputs, a1_res),
                               core_ids=list(range(8)))
    return res.results


def build_b(F=None):
    nc = F if F is not None else bass.Bass("TRN2", target_bir_lowering=False)
    precast = getattr(F, "precast", False)
    x1_d = nc.dram_tensor("x1f", [T, D], F32, kind="ExternalInput")
    wk_d = nc.dram_tensor("wk1", [1024, D], F32, kind="ExternalInput")
    wv_d = nc.dram_tensor("wv1", [1024, D], F32, kind="ExternalInput")
    lnw_d = nc.dram_tensor("lnw", [D, 1], F32, kind="ExternalInput")
    lnb_d = nc.dram_tensor("lnb", [D, 1], F32, kind="ExternalInput")
    mk_d = nc.dram_tensor("mixk", [D, 1], F32, kind="ExternalInput")
    mv_d = nc.dram_tensor("mixv", [D, 1], F32, kind="ExternalInput")
    td_d = nc.dram_tensor("tdec", [1024, 1], F32, kind="ExternalInput")
    cw_d = nc.dram_tensor("ctw", [1024, 1], F32, kind="ExternalInput")
    ident_d = nc.dram_tensor("ident", [128, 128], BF16, kind="ExternalInput")

    xln_out = nc.dram_tensor("xln_out", [D, T + 1], BF16, kind="ExternalOutput")
    wkv_out = nc.dram_tensor("wkv_out", [1024, T], BF16, kind="ExternalOutput")

    wk_bf = nc.dram_tensor("wk_bf", [1024, D], BF16, kind="Internal")
    wv_bf = nc.dram_tensor("wv_bf", [1024, D], BF16, kind="Internal")

    with ExitStack() as ctx:
        sb = lambda *a: ctx.enter_context(nc.sbuf_tensor(*a))
        gq = Sem(ctx, nc, "gq")
        sq = Sem(ctx, nc, "sq")
        pe = Sem(ctx, nc, "pe")
        ve = Sem(ctx, nc, "ve")
        se = Sem(ctx, nc, "se")

        xln = [sb(f"xln_{p}", [128, T + 1], BF16) for p in range(16)]
        ident = sb("identb", [128, 128], BF16)
        lnw = [sb(f"lnw_{p}", [128, 1], F32) for p in range(16)]
        lnb = [sb(f"lnb_{p}", [128, 1], F32) for p in range(16)]
        mk = [sb(f"mk_{p}", [128, 1], F32) for p in range(16)]
        mv = [sb(f"mv_{p}", [128, 1], F32) for p in range(16)]
        tdc = [sb(f"td_{s}", [128, 1], F32) for s in range(8)]
        cwc = [sb(f"cw_{s}", [128, 1], F32) for s in range(8)]
        acol = [sb(f"ac_{s}", [128, 1], F32) for s in range(8)]

        with nc.Block() as blk:
            @blk.gpsimd
            def _(g):
                gv = VW(g, gq)
                if not precast:
                    gq.dma(g.dma_start(wk_bf[:], wk_d[:]))
                    gq.dma(g.dma_start(wv_bf[:], wv_d[:]))
                gq.dma(g.dma_start(ident[:], ident_d[:]))
                for p in range(16):
                    sl = slice(p * 128, (p + 1) * 128)
                    gq.dma(g.dma_start(lnw[p][:], lnw_d[sl, :]))
                    gq.dma(g.dma_start(lnb[p][:], lnb_d[sl, :]))
                    gq.dma(g.dma_start(mk[p][:], mk_d[sl, :]))
                    gq.dma(g.dma_start(mv[p][:], mv_d[sl, :]))
                for s_ in range(8):
                    sl = slice(s_ * 128, (s_ + 1) * 128)
                    gq.dma(g.dma_start(tdc[s_][:], td_d[sl, :]))
                    gq.dma(g.dma_start(cwc[s_][:], cw_d[sl, :]))
                for p in range(16):
                    gv.memset(xln[p][:, 0:1], 0.0)
                g.wait_ge(gq.h, gq.n)

        with nc.Block() as blk:
            @blk.scalar
            def _(sc):
                for s_ in range(8):
                    sc.activation(acol[s_][:], tdc[s_][:], AF.Exp)
        with nc.Block() as blk:
            @blk.vector
            def _(v):
                v = VW(v, ve)
                for s_ in range(8):
                    v.tensor_single_scalar(acol[s_][:], acol[s_][:], -1.0,
                                           ALU.mult)
        with nc.Block() as blk:
            @blk.scalar
            def _(sc):
                for s_ in range(8):
                    sc.activation(acol[s_][:], acol[s_][:], AF.Exp)

        emit_ln_to_fm(nc, x1_d, xln, T, ident, sq, pe, ve, fm_col0=1,
                      inner=True, ln_w=lnw, ln_b=lnb)

        with nc.Block() as blk:
            @blk.gpsimd
            def _(g):
                for p in range(16):
                    gq.dma(g.dma_start(xln_out[p * 128:(p + 1) * 128, :],
                                       xln[p][:, 0:T + 1]))

        kres = [sb(f"kres_{m}", [128, T], BF16) for m in range(8)]
        vres = [sb(f"vres_{m}", [128, T], BF16) for m in range(8)]

        with ExitStack() as c2:
            sb2 = lambda *a: c2.enter_context(nc.sbuf_tensor(*a))
            wkT = [sb2(f"wkT_{k}", [128, 1024], BF16) for k in range(16)]
            wvT = [sb2(f"wvT_{k}", [128, 1024], BF16) for k in range(16)]
            xkb = [sb2(f"xkb_{i}", [128, 256], BF16) for i in range(2)]
            xvb = [sb2(f"xvb_{i}", [128, 256], BF16) for i in range(2)]
            dbb = [sb2(f"dbb_{i}", [128, 256], BF16) for i in range(2)]
            ps = [c2.enter_context(nc.psum_tensor(f"kv_ps{i}", [128, 512], F32))
                  for i in range(8)]
            with nc.Block() as blk:
                @blk.sync
                def _(s):
                    for kp in range(16):
                        sq.dma(s.dma_start_transpose(
                            wkT[kp][:], wk_bf[:, kp * 128:(kp + 1) * 128]))
                    for kp in range(16):
                        sq.dma(s.dma_start_transpose(
                            wvT[kp][:], wv_bf[:, kp * 128:(kp + 1) * 128]))
                    s.wait_ge(sq.h, sq.n)

            for tc in range(8):
                with nc.Block() as blk:
                    pe0, ve0 = pe.n, ve.n

                    @blk.tensor
                    def _(t):
                        for kp in range(16):
                            t.wait_ge(ve.h, ve0 + kp * 3 + 3)
                            first = kp == 0
                            last = kp == 15
                            for m in range(8):
                                bank, colr = m % 4, (m // 4) * 256
                                t.matmul(ps[bank][:, colr:colr + 256],
                                         wkT[kp][:, m * 128:(m + 1) * 128],
                                         xkb[kp % 2][:],
                                         start=(first and m < 4), stop=False)
                            for m in range(8):
                                bank, colr = 4 + m % 4, (m // 4) * 256
                                mm = t.matmul(ps[bank][:, colr:colr + 256],
                                              wvT[kp][:, m * 128:(m + 1) * 128],
                                              xvb[kp % 2][:],
                                              start=(first and m < 4),
                                              stop=(last and m == 7))
                            pe.inc(mm)

                    @blk.vector
                    def _(v):
                        v = VW(v, ve)
                        for kp in range(16):
                            if kp >= 2:
                                v.wait_ge(pe.h, pe0 + kp - 1)
                            xs = xln[kp][:, 1 + tc * 256: 1 + (tc + 1) * 256]
                            sx = xln[kp][:, tc * 256: (tc + 1) * 256]
                            v.tensor_sub(dbb[kp % 2][:], xs, sx)
                            v.scalar_tensor_tensor(xkb[kp % 2][:], dbb[kp % 2][:],
                                                   mk[kp][:], sx,
                                                   ALU.mult, ALU.add)
                            v.scalar_tensor_tensor(xvb[kp % 2][:], dbb[kp % 2][:],
                                                   mv[kp][:], sx,
                                                   ALU.mult, ALU.add)
                        v.wait_ge(pe.h, pe0 + 16)
                        for m in range(8):
                            bank, colr = m % 4, (m // 4) * 256
                            v.tensor_copy(kres[m][:, tc * 256:(tc + 1) * 256],
                                          ps[bank][:, colr:colr + 256])
                        for m in range(8):
                            bank, colr = 4 + m % 4, (m // 4) * 256
                            v.tensor_copy(vres[m][:, tc * 256:(tc + 1) * 256],
                                          ps[bank][:, colr:colr + 256])

        with ExitStack() as c2:
            sb2 = lambda *a: c2.enter_context(nc.sbuf_tensor(*a))
            bcast = sb2("bcast", [128, T], F32)
            ek = sb2("ek", [128, T], BF16)
            ekw = sb2("ekw", [128, T], BF16)
            gg = sb2("gg", [128, T], BF16)
            hh = sb2("hh", [128, T], BF16)
            numb = sb2("numb", [128, T + 1], F32)
            denb = sb2("denb", [128, T + 1], F32)
            nm = sb2("nm", [128, T], F32)
            dn = sb2("dn", [128, T], F32)
            wkvt = sb2("wkvt", [128, T], BF16)
            for s_ in range(8):
                with nc.Block() as blk:
                    @blk.scalar
                    def _(sc):
                        sc.activation(ek[:], kres[s_][:], AF.Exp)
                        sc.activation(ekw[:], kres[s_][:], AF.Exp,
                                      bias=cwc[s_][:])
                with nc.Block() as blk:
                    @blk.vector
                    def _(v):
                        v = VW(v, ve)
                        v.memset(bcast[:], 1.0)
                        v.tensor_single_scalar(bcast[:], bcast[:], acol[s_][:],
                                               ALU.mult)
                        v.tensor_mul(gg[:], ek[:], vres[s_][:])
                        v.tensor_mul(hh[:], ekw[:], vres[s_][:])
                        v.memset(numb[:, 0:1], 0.0)
                        v.memset(denb[:, 0:1], 0.0)
                        v.tensor_tensor_scan(numb[:, 1:T + 1], bcast[:], gg[:],
                                             0.0, ALU.mult, ALU.add)
                        v.tensor_tensor_scan(denb[:, 1:T + 1], bcast[:], ek[:],
                                             0.0, ALU.mult, ALU.add)
                        v.tensor_add(nm[:], numb[:, 0:T], hh[:])
                        v.tensor_add(dn[:], denb[:, 0:T], ekw[:])
                        v.reciprocal(dn[:], dn[:])
                        v.tensor_mul(wkvt[:], nm[:], dn[:])
                with nc.Block() as blk:
                    @blk.gpsimd
                    def _(g):
                        gq.dma(g.dma_start(
                            wkv_out[s_ * 128:(s_ + 1) * 128, :], wkvt[:]))
                        g.wait_ge(gq.h, gq.n)

        with nc.Block() as blk:
            @blk.gpsimd
            def _(g):
                g.wait_ge(gq.h, gq.n)
    return nc


def _in_maps_b(inputs, a2_res):
    col = lambda v: np.ascontiguousarray(np.asarray(v).reshape(-1, 1))
    ident = np.eye(128, dtype=np.float32).astype(ml_dtypes.bfloat16)
    wk = np.asarray(inputs["L1_attn_Wk"])
    wv = np.asarray(inputs["L1_attn_Wv"])
    td = np.asarray(inputs["L1_attn_time_decay"])
    cw = np.asarray(inputs["L1_attn_current_token_weight"])
    base = {
        "lnw": col(inputs["L1_attn_ln_w"]),
        "lnb": col(inputs["L1_attn_ln_b"]),
        "mixk": col(inputs["L1_attn_mix_k"]),
        "mixv": col(inputs["L1_attn_mix_v"]),
        "ident": ident,
    }
    in_maps = []
    for c in range(8):
        b, half = c // 2, c % 2
        rows = slice(half * 1024, half * 1024 + 1024)
        x1_full = np.concatenate([np.asarray(a2_res[2 * b]["x1_out"]),
                                  np.asarray(a2_res[2 * b + 1]["x1_out"])],
                                 axis=0)
        m = dict(base)
        m["x1f"] = np.ascontiguousarray(x1_full)
        m["wk1"] = np.ascontiguousarray(wk[rows])
        m["wv1"] = np.ascontiguousarray(wv[rows])
        m["tdec"] = col(td[rows])
        m["ctw"] = col(cw[rows])
        in_maps.append(m)
    return in_maps


def run_b(inputs, a2_res):
    nc = build_b()
    res = run_bass_kernel_spmd(nc, _in_maps_b(inputs, a2_res),
                               core_ids=list(range(8)))
    return res.results


def build_c(F=None):
    nc = F if F is not None else bass.Bass("TRN2", target_bir_lowering=False)
    precast = getattr(F, "precast", False)
    x1_d = nc.dram_tensor("x1own", [HTOK, D], F32, kind="ExternalInput")
    xlnE_d = nc.dram_tensor("xlnE", [D, HTOK + 1], BF16, kind="ExternalInput")
    wkvo_d = nc.dram_tensor("wkvo", [D, HTOK], BF16, kind="ExternalInput")
    wr_d = nc.dram_tensor("wr1", [D, D], F32, kind="ExternalInput")
    wo_d = nc.dram_tensor("wo1", [D, D], F32, kind="ExternalInput")
    mr_d = nc.dram_tensor("mixr", [D, 1], F32, kind="ExternalInput")
    gw_d = nc.dram_tensor("gate_w1", [D, 8], F32, kind="ExternalInput")
    wgu_d = nc.dram_tensor("wgu", [8, D, 2048], F32, kind="ExternalInput")
    wdn_d = nc.dram_tensor("wdn", [8, 1024, D], F32, kind="ExternalInput")
    swu_d = nc.dram_tensor("swu", [D, 8192], F32, kind="ExternalInput")
    swd_d = nc.dram_tensor("swd", [4096, D], F32, kind="ExternalInput")
    ident_d = nc.dram_tensor("ident", [128, 128], BF16, kind="ExternalInput")

    out_d = nc.dram_tensor("out_own", [HTOK, D], F32, kind="ExternalOutput")

    wr_bf = nc.dram_tensor("wr_bf", [D, D], BF16, kind="Internal")
    wo_bf = nc.dram_tensor("wo_bf", [D, D], BF16, kind="Internal")
    xn1tm_d = nc.dram_tensor("xn1tm_d", [HTOK, D], BF16, kind="Internal")
    s1_d = nc.dram_tensor("s1_d", [HTOK, D], F32, kind="Internal")

    with ExitStack() as ctx:
        sb = lambda *a: ctx.enter_context(nc.sbuf_tensor(*a))
        gq = Sem(ctx, nc, "gq")
        sq = Sem(ctx, nc, "sq")
        pe = Sem(ctx, nc, "pe")
        ve = Sem(ctx, nc, "ve")
        se = Sem(ctx, nc, "se")

        xn1 = [sb(f"xn1_{p}", [128, HTOK], BF16) for p in range(16)]
        gates = [sb(f"gates_{t}", [128, 8], F32) for t in range(8)]

        if not precast:
            with nc.Block() as blk:
                @blk.gpsimd
                def _(g):
                    gq.dma(g.dma_start(wr_bf[:], wr_d[:]))
                    gq.dma(g.dma_start(wo_bf[:], wo_d[:]))
                    g.wait_ge(gq.h, gq.n)

        with ExitStack() as c0:
            ident = c0.enter_context(nc.sbuf_tensor("identb", [128, 128], BF16))
            with nc.Block() as blk:
                @blk.gpsimd
                def _(g):
                    gq.dma(g.dma_start(ident[:], ident_d[:]))
                    g.wait_ge(gq.h, gq.n)
            ktm = [c0.enter_context(nc.sbuf_tensor(f"ktm_{t}", [128, D], BF16))
                   for t in range(8)]
            emit_ln_to_fm(nc, x1_d, xn1, HTOK, ident, sq, pe, ve, keep_tm=ktm)
            with nc.Block() as blk:
                @blk.gpsimd
                def _(g):
                    for t in range(8):
                        gq.dma(g.dma_start(xn1tm_d[t * 128:(t + 1) * 128, :],
                                           ktm[t][:]))
                    g.wait_ge(gq.h, gq.n)

        with ExitStack() as cA:
            sbA = lambda *a: cA.enter_context(nc.sbuf_tensor(*a))
            wkvo = [sbA(f"wkvo_{p}", [128, HTOK], BF16) for p in range(16)]
            rr = [sbA(f"rr_{p}", [128, HTOK], BF16) for p in range(16)]
            with nc.Block() as blk:
                @blk.gpsimd
                def _(g):
                    for p in range(16):
                        gq.dma(g.dma_start(wkvo[p][:],
                                           wkvo_d[p * 128:(p + 1) * 128, :]))
                    g.wait_ge(gq.h, gq.n)

            with ExitStack() as cB:
                sbB = lambda *a: cB.enter_context(nc.sbuf_tensor(*a))
                xlnE = [sbB(f"xlnE_{p}", [128, HTOK + 1], BF16)
                        for p in range(16)]
                mr = [sbB(f"mr_{p}", [128, 1], F32) for p in range(16)]
                wrT = [sbB(f"wrT_{k}", [128, D], BF16) for k in range(16)]
                xrb = [sbB(f"xrb_{i}", [128, 256], BF16) for i in range(2)]
                drb = [sbB(f"drb_{i}", [128, 256], BF16) for i in range(2)]
                ps = [cB.enter_context(
                    nc.psum_tensor(f"r_ps{i}", [128, 512], F32))
                    for i in range(8)]
                with nc.Block() as blk:
                    @blk.gpsimd
                    def _(g):
                        for p in range(16):
                            gq.dma(g.dma_start(
                                xlnE[p][:], xlnE_d[p * 128:(p + 1) * 128, :]))
                            gq.dma(g.dma_start(
                                mr[p][:], mr_d[p * 128:(p + 1) * 128, :]))
                        g.wait_ge(gq.h, gq.n)

                    @blk.sync
                    def _(s):
                        for kp in range(16):
                            sq.dma(s.dma_start_transpose(
                                wrT[kp][:], wr_bf[:, kp * 128:(kp + 1) * 128]))
                        s.wait_ge(sq.h, sq.n)

                for tc in range(4):
                    with nc.Block() as blk:
                        pe0, ve0 = pe.n, ve.n

                        @blk.tensor
                        def _(t):
                            for kp in range(16):
                                t.wait_ge(ve.h, ve0 + kp * 2 + 2)
                                first, last = kp == 0, kp == 15
                                for m in range(16):
                                    bank, colr = m % 8, (m // 8) * 256
                                    mm = t.matmul(
                                        ps[bank][:, colr:colr + 256],
                                        wrT[kp][:, m * 128:(m + 1) * 128],
                                        xrb[kp % 2][:],
                                        start=(first and m < 8),
                                        stop=(last and m == 15))
                                pe.inc(mm)

                        @blk.vector
                        def _(v):
                            v = VW(v, ve)
                            for kp in range(16):
                                if kp >= 2:
                                    v.wait_ge(pe.h, pe0 + kp - 1)
                                xs = xlnE[kp][:, 1 + tc * 256:
                                              1 + (tc + 1) * 256]
                                sx = xlnE[kp][:, tc * 256:(tc + 1) * 256]
                                v.tensor_sub(drb[kp % 2][:], xs, sx)
                                v.scalar_tensor_tensor(
                                    xrb[kp % 2][:], drb[kp % 2][:],
                                    mr[kp][:], sx, ALU.mult, ALU.add)
                            v.wait_ge(pe.h, pe0 + 16)
                            for m in range(16):
                                bank, colr = m % 8, (m // 8) * 256
                                v.tensor_copy(
                                    rr[m][:, tc * 256:(tc + 1) * 256],
                                    ps[bank][:, colr:colr + 256])

            with nc.Block() as blk:
                @blk.scalar
                def _(sc):
                    for p in range(16):
                        sc.activation(rr[p][:], rr[p][:], AF.Sigmoid)
            with nc.Block() as blk:
                @blk.vector
                def _(v):
                    v = VW(v, ve)
                    for p in range(16):
                        v.tensor_mul(rr[p][:], rr[p][:], wkvo[p][:])

            with ExitStack() as cC:
                sbC = lambda *a: cC.enter_context(nc.sbuf_tensor(*a))
                woT = [sbC(f"woT_{k}", [128, D], BF16) for k in range(16)]
                s1t = [sbC(f"s1t_{i}", [128, D], F32) for i in range(2)]
                ps = [cC.enter_context(
                    nc.psum_tensor(f"wo_ps{i}", [128, 512], F32))
                    for i in range(8)]
                with nc.Block() as blk:
                    @blk.sync
                    def _(s):
                        for kp in range(16):
                            sq.dma(s.dma_start_transpose(
                                woT[kp][:], wo_bf[:, kp * 128:(kp + 1) * 128]))
                        s.wait_ge(sq.h, sq.n)
                with nc.Block() as blk:
                    pe0, ve0, gq0 = pe.n, ve.n, gq.n

                    @blk.tensor
                    def _(t):
                        for tt in range(8):
                            bs = (tt % 2) * 4
                            if tt >= 2:
                                t.wait_ge(ve.h, ve0 + (tt - 2) * 4 + 4)
                            for kp in range(16):
                                for oc in range(4):
                                    mm = t.matmul(
                                        ps[bs + oc][:],
                                        rr[kp][:, tt * 128:(tt + 1) * 128],
                                        woT[kp][:, oc * 512:(oc + 1) * 512],
                                        start=(kp == 0), stop=(kp == 15))
                            pe.inc(mm)

                    @blk.vector
                    def _(v):
                        v = VW(v, ve)
                        for tt in range(8):
                            bs = (tt % 2) * 4
                            v.wait_ge(pe.h, pe0 + tt + 1)
                            if tt >= 2:
                                v.wait_ge(gq.h, gq0 + (tt - 1) * 16)
                            for oc in range(4):
                                v.tensor_copy(
                                    s1t[tt % 2][:, oc * 512:(oc + 1) * 512],
                                    ps[bs + oc][:])

                    @blk.gpsimd
                    def _(g):
                        for tt in range(8):
                            g.wait_ge(ve.h, ve0 + tt * 4 + 4)
                            gq.dma(g.dma_start(
                                s1_d[tt * 128:(tt + 1) * 128, :],
                                s1t[tt % 2][:]))
                            g.wait_ge(gq.h, gq.n)

        acc = [sb(f"acc_{t}", [128, D], F32) for t in range(8)]
        with nc.Block() as blk:
            @blk.gpsimd
            def _(g):
                gv = VW(g, gq)
                for t in range(8):
                    gv.memset(acc[t][:], 0.0)

        with ExitStack() as cG:
            psg = [cG.enter_context(nc.psum_tensor("g_ps", [128, 64], F32))]
            gwb = [cG.enter_context(nc.sbuf_tensor(f"gwb_{p}", [128, 8], BF16))
                   for p in range(16)]
            lg = [cG.enter_context(nc.sbuf_tensor(f"lg{t}", [128, 8], F32))
                  for t in range(8)]
            m1 = [cG.enter_context(nc.sbuf_tensor(f"m1_{t}", [128, 1], F32))
                  for t in range(8)]
            nm1 = [cG.enter_context(nc.sbuf_tensor(f"nm1_{t}", [128, 1], F32))
                   for t in range(8)]
            ee = [cG.enter_context(nc.sbuf_tensor(f"ee{t}", [128, 8], F32))
                  for t in range(8)]
            scr = [cG.enter_context(nc.sbuf_tensor(f"sc{t}", [128, 8], F32))
                   for t in range(8)]
            zz = [cG.enter_context(nc.sbuf_tensor(f"zz{t}", [128, 1], F32))
                  for t in range(8)]
            with nc.Block() as blk:
                @blk.gpsimd
                def _(g):
                    for p in range(16):
                        gq.dma(g.dma_start(gwb[p][:],
                                           gw_d[p * 128:(p + 1) * 128, :]))
                    g.wait_ge(gq.h, gq.n)
            with nc.Block() as blk:
                pe0, ve0 = pe.n, ve.n

                @blk.tensor
                def _(t):
                    for tt in range(8):
                        for kp in range(16):
                            mm = t.matmul(psg[0][:, tt * 8:(tt + 1) * 8],
                                          xn1[kp][:, tt * 128:(tt + 1) * 128],
                                          gwb[kp][:],
                                          start=(tt == 0 and kp == 0),
                                          stop=(tt == 7 and kp == 15))
                    pe.inc(mm)

                @blk.vector
                def _(v):
                    v = VW(v, ve)
                    v.wait_ge(pe.h, pe0 + 1)
                    for tt in range(8):
                        v.tensor_copy(lg[tt][:], psg[0][:, tt * 8:(tt + 1) * 8])
                        v.reduce_max(m1[tt][:], lg[tt][:], axis=AX)
                        v.tensor_single_scalar(nm1[tt][:], m1[tt][:], -1.0,
                                               ALU.mult)
            with nc.Block() as blk:
                @blk.scalar
                def _(s):
                    for tt in range(8):
                        s.activation(ee[tt][:], lg[tt][:], AF.Exp,
                                     bias=nm1[tt][:])
            with nc.Block() as blk:
                @blk.vector
                def _(v):
                    v = VW(v, ve)
                    for tt in range(8):
                        v.tensor_single_scalar(scr[tt][:], lg[tt][:], m1[tt][:],
                                               ALU.is_ge)
                        v.scalar_tensor_tensor(scr[tt][:], scr[tt][:], -1e9,
                                               lg[tt][:], ALU.mult, ALU.add)
                        v.reduce_max(zz[tt][:], scr[tt][:], axis=AX)
                        v.tensor_single_scalar(scr[tt][:], lg[tt][:], zz[tt][:],
                                               ALU.is_ge)
                        v.tensor_mul(scr[tt][:], scr[tt][:], ee[tt][:])
                        v.reduce_sum(zz[tt][:], scr[tt][:], axis=AX)
                        v.reciprocal(zz[tt][:], zz[tt][:])
                        v.tensor_single_scalar(gates[tt][:], scr[tt][:],
                                               zz[tt][:], ALU.mult)

        with ExitStack() as cF:
            sbF = lambda *a: cF.enter_context(nc.sbuf_tensor(*a))
            wbig = [sbF(f"wbig_{k}", [128, 2048], BF16) for k in range(16)]
            actf = [sbF(f"actf_{m}", [128, HTOK], BF16) for m in range(8)]
            actu = [sbF(f"actu_{m}", [128, HTOK], BF16) for m in range(8)]
            ps = [cF.enter_context(
                nc.psum_tensor(f"ff_ps{i}", [128, 512], F32))
                for i in range(8)]
            with nc.Block() as blk:
                @blk.gpsimd
                def _(g):
                    for kp in range(16):
                        gq.dma(g.dma_start(
                            wbig[kp][:], wgu_d[0, kp * 128:(kp + 1) * 128, :]))
                    g.wait_ge(gq.h, gq.n)
            for e in range(8):
                with nc.Block() as blk:
                    pe0, ve0 = pe.n, ve.n
                    jobs = [(m, tc) for m in range(16) for tc in range(2)]

                    @blk.tensor
                    def _(t):
                        for ji, (m, tc) in enumerate(jobs):
                            bank = ji % 8
                            if ji >= 8:
                                t.wait_ge(ve.h, ve0 + ji - 7)
                            for kp in range(16):
                                mm = t.matmul(
                                    ps[bank][:],
                                    wbig[kp][:, m * 128:(m + 1) * 128],
                                    xn1[kp][:, tc * 512:(tc + 1) * 512],
                                    start=(kp == 0), stop=(kp == 15))
                            pe.inc(mm)

                    @blk.vector
                    def _(v):
                        v = VW(v, ve)
                        for ji, (m, tc) in enumerate(jobs):
                            bank = ji % 8
                            v.wait_ge(pe.h, pe0 + ji + 1)
                            dst = actf[m] if m < 8 else actu[m - 8]
                            v.tensor_copy(dst[:, tc * 512:(tc + 1) * 512],
                                          ps[bank][:])
                with nc.Block() as blk:
                    @blk.scalar
                    def _(sc):
                        for m in range(8):
                            sc.activation(actf[m][:], actf[m][:], AF.Silu)

                    @blk.gpsimd
                    def _(g):
                        for m in range(8):
                            gq.dma(g.dma_start(
                                wbig[m][:], wdn_d[e, m * 128:(m + 1) * 128, :]))
                        g.wait_ge(gq.h, gq.n)
                with nc.Block() as blk:
                    @blk.vector
                    def _(v):
                        v = VW(v, ve)
                        for m in range(8):
                            v.tensor_mul(actf[m][:], actf[m][:], actu[m][:])
                with nc.Block() as blk:
                    pe0, ve0 = pe.n, ve.n
                    jobs = [(tt, oc) for tt in range(8) for oc in range(4)]

                    @blk.gpsimd
                    def _(g):
                        for kp in range(8, 16):
                            if e + 1 < 8:
                                gq.dma(g.dma_start(
                                    wbig[kp][:],
                                    wgu_d[e + 1, kp * 128:(kp + 1) * 128, :]))
                            else:
                                gq.dma(g.dma_start(
                                    wbig[kp][:, 0:1024],
                                    swu_d[kp * 128:(kp + 1) * 128, 0:1024]))
                                gq.dma(g.dma_start(
                                    wbig[kp][:, 1024:2048],
                                    swu_d[kp * 128:(kp + 1) * 128, 4096:5120]))
                        g.wait_ge(gq.h, gq.n)

                    @blk.tensor
                    def _(t):
                        for ji, (tt, oc) in enumerate(jobs):
                            bank = ji % 8
                            if ji >= 8:
                                t.wait_ge(ve.h, ve0 + ji - 7)
                            for m in range(8):
                                mm = t.matmul(
                                    ps[bank][:],
                                    actf[m][:, tt * 128:(tt + 1) * 128],
                                    wbig[m][:, oc * 512:(oc + 1) * 512],
                                    start=(m == 0), stop=(m == 7))
                            pe.inc(mm)

                    @blk.vector
                    def _(v):
                        v = VW(v, ve)
                        for ji, (tt, oc) in enumerate(jobs):
                            bank = ji % 8
                            v.wait_ge(pe.h, pe0 + ji + 1)
                            v.scalar_tensor_tensor(
                                acc[tt][:, oc * 512:(oc + 1) * 512],
                                ps[bank][:], gates[tt][:, e:e + 1],
                                acc[tt][:, oc * 512:(oc + 1) * 512],
                                ALU.mult, ALU.add)
                with nc.Block() as blk:
                    @blk.gpsimd
                    def _(g):
                        for kp in range(0, 8):
                            if e + 1 < 8:
                                gq.dma(g.dma_start(
                                    wbig[kp][:],
                                    wgu_d[e + 1, kp * 128:(kp + 1) * 128, :]))
                            else:
                                gq.dma(g.dma_start(
                                    wbig[kp][:, 0:1024],
                                    swu_d[kp * 128:(kp + 1) * 128, 0:1024]))
                                gq.dma(g.dma_start(
                                    wbig[kp][:, 1024:2048],
                                    swu_d[kp * 128:(kp + 1) * 128, 4096:5120]))
                        g.wait_ge(gq.h, gq.n)

            for hc in range(4):
                with nc.Block() as blk:
                    pe0, ve0 = pe.n, ve.n
                    jobs = [(m, tc) for m in range(16) for tc in range(2)]

                    @blk.tensor
                    def _(t):
                        for ji, (m, tc) in enumerate(jobs):
                            bank = ji % 8
                            if ji >= 8:
                                t.wait_ge(ve.h, ve0 + ji - 7)
                            colw = (m % 8) * 128 + (m // 8) * 1024
                            for kp in range(16):
                                mm = t.matmul(
                                    ps[bank][:],
                                    wbig[kp][:, colw:colw + 128],
                                    xn1[kp][:, tc * 512:(tc + 1) * 512],
                                    start=(kp == 0), stop=(kp == 15))
                            pe.inc(mm)

                    @blk.vector
                    def _(v):
                        v = VW(v, ve)
                        for ji, (m, tc) in enumerate(jobs):
                            bank = ji % 8
                            v.wait_ge(pe.h, pe0 + ji + 1)
                            dst = actf[m] if m < 8 else actu[m - 8]
                            v.tensor_copy(dst[:, tc * 512:(tc + 1) * 512],
                                          ps[bank][:])
                with nc.Block() as blk:
                    @blk.scalar
                    def _(sc):
                        for m in range(8):
                            sc.activation(actf[m][:], actf[m][:], AF.Silu)

                    @blk.gpsimd
                    def _(g):
                        for m in range(8):
                            gq.dma(g.dma_start(
                                wbig[m][:],
                                swd_d[hc * 1024 + m * 128:
                                      hc * 1024 + (m + 1) * 128, :]))
                        g.wait_ge(gq.h, gq.n)
                with nc.Block() as blk:
                    @blk.vector
                    def _(v):
                        v = VW(v, ve)
                        for m in range(8):
                            v.tensor_mul(actf[m][:], actf[m][:], actu[m][:])
                with nc.Block() as blk:
                    pe0, ve0 = pe.n, ve.n
                    jobs = [(tt, oc) for tt in range(8) for oc in range(4)]

                    @blk.gpsimd
                    def _(g):
                        if hc + 1 < 4:
                            nxt = hc + 1
                            for kp in range(8, 16):
                                gq.dma(g.dma_start(
                                    wbig[kp][:, 0:1024],
                                    swu_d[kp * 128:(kp + 1) * 128,
                                          nxt * 1024:(nxt + 1) * 1024]))
                                gq.dma(g.dma_start(
                                    wbig[kp][:, 1024:2048],
                                    swu_d[kp * 128:(kp + 1) * 128,
                                          4096 + nxt * 1024:
                                          4096 + (nxt + 1) * 1024]))
                            g.wait_ge(gq.h, gq.n)

                    @blk.tensor
                    def _(t):
                        for ji, (tt, oc) in enumerate(jobs):
                            bank = ji % 8
                            if ji >= 8:
                                t.wait_ge(ve.h, ve0 + ji - 7)
                            for m in range(8):
                                mm = t.matmul(
                                    ps[bank][:],
                                    actf[m][:, tt * 128:(tt + 1) * 128],
                                    wbig[m][:, oc * 512:(oc + 1) * 512],
                                    start=(m == 0), stop=(m == 7))
                            pe.inc(mm)

                    @blk.vector
                    def _(v):
                        v = VW(v, ve)
                        for ji, (tt, oc) in enumerate(jobs):
                            bank = ji % 8
                            v.wait_ge(pe.h, pe0 + ji + 1)
                            v.tensor_add(acc[tt][:, oc * 512:(oc + 1) * 512],
                                         acc[tt][:, oc * 512:(oc + 1) * 512],
                                         ps[bank][:])
                if hc + 1 < 4:
                    with nc.Block() as blk:
                        @blk.gpsimd
                        def _(g):
                            nxt = hc + 1
                            for kp in range(0, 8):
                                gq.dma(g.dma_start(
                                    wbig[kp][:, 0:1024],
                                    swu_d[kp * 128:(kp + 1) * 128,
                                          nxt * 1024:(nxt + 1) * 1024]))
                                gq.dma(g.dma_start(
                                    wbig[kp][:, 1024:2048],
                                    swu_d[kp * 128:(kp + 1) * 128,
                                          4096 + nxt * 1024:
                                          4096 + (nxt + 1) * 1024]))
                            g.wait_ge(gq.h, gq.n)

        with ExitStack() as c5:
            g1t = c5.enter_context(nc.sbuf_tensor("g1t", [128, D], F32))
            xnb = c5.enter_context(nc.sbuf_tensor("xnb5", [128, D], BF16))
            rwk = c5.enter_context(nc.sbuf_tensor("rwk", [128, D], F32))
            x1t = c5.enter_context(nc.sbuf_tensor("x1t5", [128, D], F32))
            ot = c5.enter_context(nc.sbuf_tensor("ot", [128, D], F32))
            for tt in range(8):
                with nc.Block() as blk:
                    @blk.sync
                    def _(s):
                        sl = slice(tt * 128, (tt + 1) * 128)
                        sq.dma(s.dma_start(xnb[:], xn1tm_d[sl, :]))
                        sq.dma(s.dma_start(rwk[:], s1_d[sl, :]))
                        sq.dma(s.dma_start(x1t[:], x1_d[sl, :]))
                        s.wait_ge(sq.h, sq.n)

                    @blk.scalar
                    def _(sc):
                        sc.activation(g1t[:], acc[tt][:], AF.Silu)
                with nc.Block() as blk:
                    @blk.vector
                    def _(v):
                        v = VW(v, ve)
                        v.tensor_add(rwk[:], rwk[:], xnb[:])
                        v.tensor_mul(rwk[:], rwk[:], g1t[:])
                        v.tensor_add(ot[:], x1t[:], rwk[:])
                with nc.Block() as blk:
                    @blk.gpsimd
                    def _(g):
                        gq.dma(g.dma_start(out_d[tt * 128:(tt + 1) * 128, :],
                                           ot[:]))
                        g.wait_ge(gq.h, gq.n)
    return nc


def _in_maps_c(inputs, b_res):
    col = lambda v: np.ascontiguousarray(np.asarray(v).reshape(-1, 1))
    ident = np.eye(128, dtype=np.float32).astype(ml_dtypes.bfloat16)
    base = {
        "wr1": np.asarray(inputs["L1_attn_Wr"]),
        "wo1": np.asarray(inputs["L1_attn_Wo"]),
        "mixr": col(inputs["L1_attn_mix_r"]),
        "gate_w1": np.asarray(inputs["L1_ffn_w_gate"]),
        "wgu": np.asarray(inputs["L1_ffn_w_gate_up"]),
        "wdn": np.asarray(inputs["L1_ffn_w_down"]),
        "swu": np.asarray(inputs["L1_ffn_w_sh_gate_up"]),
        "swd": np.asarray(inputs["L1_ffn_w_sh_down"]),
        "ident": ident,
    }
    in_maps = []
    for c in range(8):
        b, half = c // 2, c % 2
        cols = slice(half * 1024, half * 1024 + 1024)
        xln = np.asarray(b_res[c]["xln_out"])
        T0 = half * 1024
        xlnE = xln[:, T0:T0 + 1025]
        wkvo = np.concatenate([np.asarray(b_res[2 * b]["wkv_out"])[:, cols],
                               np.asarray(b_res[2 * b + 1]["wkv_out"])[:, cols]],
                              axis=0)
        m = dict(base)
        m["xlnE"] = np.ascontiguousarray(xlnE)
        m["wkvo"] = np.ascontiguousarray(wkvo)
        in_maps.append(m)
    return in_maps


def run_c(inputs, a2_res, b_res):
    nc = build_c()
    in_maps = _in_maps_c(inputs, b_res)
    for c in range(8):
        in_maps[c]["x1own"] = np.ascontiguousarray(
            np.asarray(a2_res[c]["x1_out"]))
    res = run_bass_kernel_spmd(nc, in_maps, core_ids=list(range(8)))
    return res.results


def kernel_v0(**inputs):
    a1 = run_a1(inputs)
    a2 = run_a2(inputs, a1)
    b_ = run_b(inputs, a2)
    cc = run_c(inputs, a2, b_)
    out = np.empty((B, T, D), dtype=np.float32)
    for c in range(8):
        b, half = c // 2, c % 2
        out[b, half * 1024:(half + 1) * 1024] = np.asarray(cc[c]["out_own"])
    return out


import os as _os
_os.environ.setdefault("TRN_ALLOW_ICI_ALL_TO_ALL", "1")
from concourse.aot_env import allow_ici_all_to_all


class NCP:

    def __init__(self, nc, pfx, remap):
        self._nc = nc
        self._p = pfx
        self._remap = remap

    def set_pfx(self, pfx):
        self._p = pfx

    def sbuf_tensor(self, name, *a, **k):
        return self._nc.sbuf_tensor(self._p + name, *a, **k)

    def psum_tensor(self, name, *a, **k):
        return self._nc.psum_tensor(self._p + name, *a, **k)

    def semaphore(self, name=None, num=None):
        return self._nc.semaphore(self._p + (name or "s"), num)

    def dram_tensor(self, name, shape, dt, kind="Internal"):
        if name in self._remap:
            return self._remap[name]
        return self._nc.dram_tensor(self._p + name, shape, dt, kind="Internal")

    def Block(self, *a, **k):
        return self._nc.Block(*a, **k)

    def __getattr__(self, n):
        return getattr(self._nc, n)


def build_fused():
    nc = bass.Bass("TRN2", target_bir_lowering=False)
    PAIRS = [[0, 1], [2, 3], [4, 5], [6, 7]]
    ext_in = {}

    def ein(name, shape, dt):
        ext_in[name] = nc.dram_tensor(name, shape, dt, kind="ExternalInput")
        return ext_in[name]

    ein("x0", [T, D], F32)
    ein("wq_p", [1024, D], F32)
    ein("wk_h", [256, D], F32)
    ein("wv_h", [256, D], F32)
    ein("ident", [128, 128], BF16)
    ein("stair", [128, 896], F32)
    ein("x0own", [HTOK, D], F32)
    ein("wo", [D, D], F32)
    ein("gate_w", [D, 8], F32)
    ein("w1", [8, D, 1024], F32)
    ein("b1c", [8192, 1], F32)
    ein("w2", [8, 1024, D], F32)
    ein("b2ext", [1, 9 * D], F32)
    ein("sw1", [D, D], F32)
    ein("sb1c", [D, 1], F32)
    ein("sw2", [D, D], F32)
    ein("wk1", [1024, D], F32)
    ein("wv1", [1024, D], F32)
    ein("lnw", [D, 1], F32)
    ein("lnb", [D, 1], F32)
    ein("mixk", [D, 1], F32)
    ein("mixv", [D, 1], F32)
    ein("tdec", [1024, 1], F32)
    ein("ctw", [1024, 1], F32)
    ein("wr1", [D, D], F32)
    ein("wo1", [D, D], F32)
    ein("mixr", [D, 1], F32)
    ein("gate_w1", [D, 8], F32)
    ein("wgu", [8, D, 2048], F32)
    ein("wdn", [8, 1024, D], F32)
    ein("swu", [D, 8192], F32)
    ein("swd", [4096, D], F32)

    out_d = nc.dram_tensor("out_own", [HTOK, D], F32, kind="ExternalOutput")

    xn0_d = nc.dram_tensor("f_xn0", [D, T], BF16, kind="Internal")
    attn_d = nc.dram_tensor("f_attn", [1024, T], BF16, kind="Internal")
    attn_g = nc.dram_tensor("f_attng", [2048, T], BF16, kind="Internal")
    xln_d = nc.dram_tensor("f_xln", [D, T + 1], BF16, kind="Internal")
    wkv_d = nc.dram_tensor("f_wkv", [1024, T], BF16, kind="Internal")
    x1own_d = nc.dram_tensor("f_x1own", [HTOK, D], F32, kind="Internal")
    x1g_o = nc.dram_tensor("f_x1go", [T, D], F32, kind="Internal")
    wkv_g = nc.dram_tensor("f_wkvg", [2048, T], BF16, kind="Internal")
    xn0own_d = nc.dram_tensor("f_xn0own", [D, 1024], BF16, kind="Internal")
    attno_d = nc.dram_tensor("f_attno", [2048, 1024], BF16, kind="Internal")
    xlnx_d = nc.dram_tensor("f_xlnx", [D, 1025], BF16, kind="Internal")
    wkvo_d = nc.dram_tensor("f_wkvo", [2048, 1024], BF16, kind="Internal")

    t0i = nc.dram_tensor("t0i", [1, 1], mybir.dt.int32, kind="ExternalInput")
    ext_in["t0i"] = t0i

    wo_bf_a2 = nc.dram_tensor("f_wobf2", [D, D], BF16, kind="Internal")
    wk_bf_b = nc.dram_tensor("f_wkbf", [1024, D], BF16, kind="Internal")
    wv_bf_b = nc.dram_tensor("f_wvbf", [1024, D], BF16, kind="Internal")
    wr_bf_c = nc.dram_tensor("f_wrbf", [D, D], BF16, kind="Internal")
    wo_bf_c = nc.dram_tensor("f_wobf1", [D, D], BF16, kind="Internal")

    ccs = nc.semaphore("f_cc").__enter__()
    gqs = nc.semaphore("f_gq").__enter__()
    t0s = nc.sbuf_tensor("f_t0s", [1, 1], mybir.dt.int32).__enter__()
    rt0 = nc.gpsimd.register("f_rt0").__enter__()
    ccn = [0]
    gqn = [0]
    ld0 = [False]

    def cc_block(ccs_list, sel_list, pre_sel=()):
        def sel(g, lst):
            for dst_t, src_t, nrows, rstride, ncols in lst:
                g.dma_start(
                    bass.AP(dst_t, 0, [[ncols, nrows], [1, ncols]]),
                    bass.AP(src_t, rt0, [[rstride, nrows], [1, ncols]]),
                ).then_inc(gqs, 16)
                gqn[0] += 16
        with nc.Block() as blk:
            @blk.gpsimd
            def _(g):
                if not ld0[0]:
                    g.dma_start(t0s[:], t0i[:]).then_inc(gqs, 16)
                    gqn[0] += 16
                    g.wait_ge(gqs, gqn[0])
                    g.reg_load(rt0, t0s[0:1, 0:1])
                    ld0[0] = True
                with allow_ici_all_to_all():
                    for iap, oap in ccs_list:
                        g.collective_compute(
                            "AllGather", ALU.bypass, replica_groups=PAIRS,
                            ins=[iap], outs=[oap]).then_inc(ccs, 1)
                        ccn[0] += 1
                sel(g, pre_sel)
                g.wait_ge(ccs, ccn[0])
                sel(g, sel_list)
                g.wait_ge(gqs, gqn[0])

    with nc.Block() as blk:
        @blk.gpsimd
        def _(g):
            for dst, src in [(wo_bf_a2, ext_in["wo"]),
                             (wk_bf_b, ext_in["wk1"]), (wv_bf_b, ext_in["wv1"]),
                             (wr_bf_c, ext_in["wr1"]), (wo_bf_c, ext_in["wo1"])]:
                g.dma_start(dst.ap(), src.ap()).then_inc(gqs, 16)
                gqn[0] += 16

    remap = {"x0": ext_in["x0"], "wq_p": ext_in["wq_p"],
             "wk_h": ext_in["wk_h"], "wv_h": ext_in["wv_h"],
             "ident": ext_in["ident"], "stair": ext_in["stair"],
             "attn_out": attn_d, "xn0_out": xn0_d}
    build_a1(F=NCP(nc, "A_", remap))

    cc_block([(attn_d.ap(), attn_g.ap())],
             [(attno_d, attn_g, 2048, T, 1024)],
             pre_sel=[(xn0own_d, xn0_d, D, T, 1024)])

    remap = {"xn0f": xn0own_d, "attnf": attno_d, "x0own": ext_in["x0own"],
             "wo": ext_in["wo"], "gate_w": ext_in["gate_w"],
             "w1": ext_in["w1"], "b1c": ext_in["b1c"], "w2": ext_in["w2"],
             "b2ext": ext_in["b2ext"], "sw1": ext_in["sw1"],
             "sb1c": ext_in["sb1c"], "sw2": ext_in["sw2"],
             "x1_out": x1own_d, "wo_bf": wo_bf_a2}
    f = NCP(nc, "B_", remap)
    f.precast = True
    build_a2(F=f)

    cc_block([(x1own_d.ap(), x1g_o.ap())], [])

    remap = {"x1f": x1g_o, "wk1": ext_in["wk1"], "wv1": ext_in["wv1"],
             "lnw": ext_in["lnw"], "lnb": ext_in["lnb"],
             "mixk": ext_in["mixk"], "mixv": ext_in["mixv"],
             "tdec": ext_in["tdec"], "ctw": ext_in["ctw"],
             "ident": ext_in["ident"],
             "xln_out": xln_d, "wkv_out": wkv_d,
             "wk_bf": wk_bf_b, "wv_bf": wv_bf_b}
    f = NCP(nc, "C_", remap)
    f.precast = True
    build_b(F=f)

    cc_block([(wkv_d.ap(), wkv_g.ap())],
             [(wkvo_d, wkv_g, 2048, T, 1024)],
             pre_sel=[(xlnx_d, xln_d, D, T + 1, 1025)])

    remap = {"x1own": x1own_d, "xlnE": xlnx_d, "wkvo": wkvo_d,
             "wr1": ext_in["wr1"], "wo1": ext_in["wo1"],
             "mixr": ext_in["mixr"], "gate_w1": ext_in["gate_w1"],
             "wgu": ext_in["wgu"], "wdn": ext_in["wdn"],
             "swu": ext_in["swu"], "swd": ext_in["swd"],
             "ident": ext_in["ident"], "out_own": out_d,
             "wr_bf": wr_bf_c, "wo_bf": wo_bf_c}
    f = NCP(nc, "D_", remap)
    f.precast = True
    build_c(F=f)
    return nc


def kernel_fused(**inputs):
    nc = build_fused()
    m_a1 = _in_maps_a1(inputs)
    col = lambda v: np.ascontiguousarray(np.asarray(v).reshape(-1, 1))
    x = np.asarray(inputs["L0_attn_x"])
    wk = np.asarray(inputs["L1_attn_Wk"])
    wv = np.asarray(inputs["L1_attn_Wv"])
    td = np.asarray(inputs["L1_attn_time_decay"])
    cw = np.asarray(inputs["L1_attn_current_token_weight"])
    base = {
        "wo": np.asarray(inputs["L0_attn_wo"]),
        "gate_w": np.asarray(inputs["L0_ffn_gate_w"]),
        "w1": np.asarray(inputs["L0_ffn_w1"]),
        "b1c": np.ascontiguousarray(
            np.asarray(inputs["L0_ffn_b1"]).reshape(-1, 1)),
        "w2": np.asarray(inputs["L0_ffn_w2"]),
        "b2ext": np.ascontiguousarray(np.concatenate(
            [np.asarray(inputs["L0_ffn_b2"]).reshape(-1),
             np.asarray(inputs["L0_ffn_shared_b2"]).reshape(-1)])[None, :]),
        "sw1": np.asarray(inputs["L0_ffn_shared_w1"]),
        "sb1c": col(inputs["L0_ffn_shared_b1"]),
        "sw2": np.asarray(inputs["L0_ffn_shared_w2"]),
        "lnw": col(inputs["L1_attn_ln_w"]),
        "lnb": col(inputs["L1_attn_ln_b"]),
        "mixk": col(inputs["L1_attn_mix_k"]),
        "mixv": col(inputs["L1_attn_mix_v"]),
        "mixr": col(inputs["L1_attn_mix_r"]),
        "wr1": np.asarray(inputs["L1_attn_Wr"]),
        "wo1": np.asarray(inputs["L1_attn_Wo"]),
        "gate_w1": np.asarray(inputs["L1_ffn_w_gate"]),
        "wgu": np.asarray(inputs["L1_ffn_w_gate_up"]),
        "wdn": np.asarray(inputs["L1_ffn_w_down"]),
        "swu": np.asarray(inputs["L1_ffn_w_sh_gate_up"]),
        "swd": np.asarray(inputs["L1_ffn_w_sh_down"]),
    }
    in_maps = []
    for c in range(8):
        b, half = c // 2, c % 2
        rows = slice(half * 1024, half * 1024 + 1024)
        m = dict(base)
        m.update(m_a1[c])
        m["x0own"] = np.ascontiguousarray(x[b, rows])
        m["t0i"] = np.array([[half * 1024]], dtype=np.int32)
        m["wk1"] = np.ascontiguousarray(wk[rows])
        m["wv1"] = np.ascontiguousarray(wv[rows])
        m["tdec"] = col(td[rows])
        m["ctw"] = col(cw[rows])
        in_maps.append(m)
    res = run_bass_kernel_spmd(nc, in_maps, core_ids=list(range(8)))
    out = np.empty((B, T, D), dtype=np.float32)
    for c in range(8):
        b, half = c // 2, c % 2
        out[b, half * 1024:(half + 1) * 1024] = np.asarray(
            res.results[c]["out_own"])
    return out
